# version 01 (final)
# baseline (speedup 1.0000x reference)
import math
from contextlib import ExitStack

import numpy as np
import ml_dtypes

import concourse.bass as bass
import concourse.mybir as mybir
from concourse.bass_utils import run_bass_kernel_spmd
from concourse.bass_core_rust import IndirectOffsetOnAxis

F32 = mybir.dt.float32
BF16 = mybir.dt.bfloat16
I32 = mybir.dt.int32
U32 = mybir.dt.uint32
AF = mybir.ActivationFunctionType
ALU = mybir.AluOpType
AX = mybir.AxisListType

NCORES = 8
D = 1024
KC = D // 128
EPS = 1e-5


def ceil_div(a, b):
    return (a + b - 1) // b



def build_l1_ffn(NT=4096, H=4096, TT=512):
    nc = bass.Bass("TRN2", target_bir_lowering=False, detect_race_conditions=False)
    xT = nc.dram_tensor("xT", [D, NT], F32, kind="ExternalInput").ap()
    w1 = nc.dram_tensor("w1", [D, H], F32, kind="ExternalInput").ap()
    b1 = nc.dram_tensor("b1", [H], F32, kind="ExternalInput").ap()
    w2 = nc.dram_tensor("w2", [H, D], F32, kind="ExternalInput").ap()
    b2 = nc.dram_tensor("b2", [D], F32, kind="ExternalInput").ap()
    yT = nc.dram_tensor("yT", [D, NT], F32, kind="ExternalOutput").ap()

    HC = H // 128
    NTILE = ceil_div(NT, TT)
    assert NT % TT == 0

    PE_T = HC + KC + 1 + KC + 1
    ACT_T = HC + KC + 1
    EV_T = KC + 1 + 1

    with ExitStack() as ctx:
        sb = lambda name, shape, dt=F32: ctx.enter_context(nc.sbuf_tensor(name, shape, dt))
        ps = lambda name, shape, dt=F32: ctx.enter_context(nc.psum_tensor(name, shape, dt))
        sem = lambda name: ctx.enter_context(nc.semaphore(name))

        w1_bf = sb("w1_bf", [128, KC, H], BF16)
        w2_bf = sb("w2_bf", [128, HC, D], BF16)
        b1_s = sb("b1_s", [128, HC])
        b2_s = sb("b2_s", [128, KC])
        ones128 = sb("ones128", [128, 1])
        ones1 = sb("ones1", [1, 128])
        x_f = sb("x_f", [128, KC, TT])
        x_bf = sb("x_bf", [128, KC, TT], BF16)
        h_bf = sb("h_bf", [128, HC, TT], BF16)
        sq_f = sb("sq_f", [128, TT])
        ev_tmp = sb("ev_tmp", [128, TT])
        stat_m = sb("stat_m", [1, TT])
        stat_r = sb("stat_r", [1, TT])

        pmm = [ps(f"pmm{i}", [128, TT]) for i in range(4)]
        pstat_m = ps("pstat_m", [1, TT])
        pstat_q = ps("pstat_q", [1, TT])
        pbc = ps("pbc", [128, 2 * TT])

        s_wt = sem("s_wt")
        s_ldx = sem("s_ldx")
        s_xbf = sem("s_xbf")
        s_pe = sem("s_pe")
        s_act = sem("s_act")
        s_ev = sem("s_ev")
        s_st = sem("s_st")

        N_WT = 2 + HC + KC
        block = ctx.enter_context(nc.Block())

        def store_tile(sync, t):
            sync.wait_ge(s_ev, (t + 1) * EV_T)
            sync.dma_start(
                out=yT[:, t * TT:(t + 1) * TT].rearrange("(kc p) t -> p kc t", p=128),
                in_=x_f[:],
            ).then_inc(s_st, 16)

        @block.sync
        def _(sync):
            for t in range(NTILE):
                t0 = t * TT
                if t >= 1:
                    store_tile(sync, t - 1)
                    sync.wait_ge(s_st, 16 * t)
                sync.dma_start(
                    out=x_f[:],
                    in_=xT[:, t0:t0 + TT].rearrange("(kc p) t -> p kc t", p=128),
                ).then_inc(s_ldx, 16)
            store_tile(sync, NTILE - 1)
            sync.wait_ge(s_st, 16 * NTILE)

        @block.gpsimd
        def _(gpsimd):
            gpsimd.dma_start(out=w1_bf[:],
                             in_=w1.rearrange("(kc p) h -> p kc h", p=128)).then_inc(s_wt, 16)
            gpsimd.dma_start(out=w2_bf[:],
                             in_=w2.rearrange("(hc p) d -> p hc d", p=128)).then_inc(s_wt, 16)
            for c in range(HC):
                gpsimd.dma_start(out=b1_s[:, c:c + 1],
                                 in_=b1[c * 128:(c + 1) * 128].unsqueeze(1)).then_inc(s_wt, 16)
            for c in range(KC):
                gpsimd.dma_start(out=b2_s[:, c:c + 1],
                                 in_=b2[c * 128:(c + 1) * 128].unsqueeze(1)).then_inc(s_wt, 16)

        @block.vector
        def _(vector):
            vector.memset(ones128[:], 1.0)
            vector.memset(ones1[:], 1.0).then_inc(s_xbf, 1)
            for t in range(NTILE):
                vector.wait_ge(s_ldx, 16 * (t + 1))
                if t >= 1:
                    vector.wait_ge(s_pe, (t - 1) * PE_T + HC)
                vector.tensor_copy(out=x_bf[:], in_=x_f[:]).then_inc(s_xbf, 1)
                for oc in range(KC):
                    vector.wait_ge(s_pe, t * PE_T + HC + oc + 1)
                    bank = pmm[oc % 4]
                    vector.tensor_scalar(out=ev_tmp[:], in0=bank[:],
                                         scalar1=b2_s[:, oc:oc + 1], scalar2=None,
                                         op0=ALU.add)
                    vector.tensor_add(out=x_f[:, oc, :], in0=ev_tmp[:],
                                      in1=x_f[:, oc, :]).then_inc(s_ev, 1)
                vector.wait_ge(s_pe, t * PE_T + HC + KC + 1 + KC)
                vector.tensor_scalar(out=stat_m[:], in0=pstat_m[:],
                                     scalar1=1.0 / D, scalar2=None, op0=ALU.mult)
                vector.tensor_scalar(out=stat_r[:], in0=pstat_q[:],
                                     scalar1=1.0 / D, scalar2=None, op0=ALU.mult)
                vector.tensor_tensor(out=ev_tmp[0:1, :], in0=stat_m[:], in1=stat_m[:], op=ALU.mult)
                vector.tensor_sub(out=stat_r[:], in0=stat_r[:], in1=ev_tmp[0:1, :])
                vector.tensor_scalar(out=stat_r[:], in0=stat_r[:],
                                     scalar1=float(EPS), scalar2=None, op0=ALU.add)
                vector.reciprocal(out=stat_r[:], in_=stat_r[:]).then_inc(s_ev, 1)
                vector.wait_ge(s_pe, (t + 1) * PE_T)
                for oc in range(KC):
                    vector.tensor_sub(out=x_f[:, oc, :], in0=x_f[:, oc, :], in1=pbc[:, 0:TT])
                    inst = vector.tensor_mul(out=x_f[:, oc, :], in0=x_f[:, oc, :], in1=pbc[:, TT:2 * TT])
                inst.then_inc(s_ev, 1)

        @block.scalar
        def _(scalar):
            for t in range(NTILE):
                for hc in range(HC):
                    scalar.wait_ge(s_pe, t * PE_T + hc + 1)
                    bank = pmm[hc % 4]
                    scalar.activation(out=h_bf[:, hc, :], in_=bank[:], func=AF.Gelu,
                                      bias=b1_s[:, hc:hc + 1], scale=1.0).then_inc(s_act, 1)
                for oc in range(KC):
                    scalar.wait_ge(s_ev, t * EV_T + oc + 1)
                    if oc >= 1:
                        scalar.wait_ge(s_pe, t * PE_T + HC + KC + 1 + oc)
                    scalar.activation(out=sq_f[:], in_=x_f[:, oc, :],
                                      func=AF.Square).then_inc(s_act, 1)
                scalar.wait_ge(s_ev, t * EV_T + KC + 1)
                scalar.activation(out=stat_r[:], in_=stat_r[:], func=AF.Sqrt).then_inc(s_act, 1)

        @block.tensor
        def _(tensor):
            tensor.wait_ge(s_wt, 16 * N_WT)
            for t in range(NTILE):
                tensor.wait_ge(s_xbf, t + 2)
                for hc in range(HC):
                    bank = pmm[hc % 4]
                    if hc >= 4 or t >= 1:
                        if hc >= 4:
                            tensor.wait_ge(s_act, t * ACT_T + hc - 3)
                        else:
                            tensor.wait_ge(s_ev, (t - 1) * EV_T + (KC - 4 + hc) + 1)
                    inst = None
                    for kc in range(KC):
                        inst = tensor.matmul(out=bank[:],
                                             lhsT=w1_bf[:, kc, hc * 128:(hc + 1) * 128],
                                             rhs=x_bf[:, kc, :],
                                             start=(kc == 0), stop=(kc == KC - 1))
                    inst.then_inc(s_pe, 1)
                tensor.wait_ge(s_act, t * ACT_T + HC)
                for oc in range(KC):
                    bank = pmm[oc % 4]
                    if oc >= 4:
                        tensor.wait_ge(s_ev, t * EV_T + oc - 3)
                    else:
                        tensor.wait_ge(s_act, t * ACT_T + HC)
                    inst = None
                    for hc in range(HC):
                        inst = tensor.matmul(out=bank[:],
                                             lhsT=w2_bf[:, hc, oc * 128:(oc + 1) * 128],
                                             rhs=h_bf[:, hc, :],
                                             start=(hc == 0), stop=(hc == HC - 1))
                    inst.then_inc(s_pe, 1)
                inst = None
                for oc in range(KC):
                    tensor.wait_ge(s_ev, t * EV_T + oc + 1)
                    inst = tensor.matmul(out=pstat_m[:], lhsT=ones128[:], rhs=x_f[:, oc, :],
                                         start=(oc == 0), stop=(oc == KC - 1))
                inst.then_inc(s_pe, 1)
                for oc in range(KC):
                    tensor.wait_ge(s_act, t * ACT_T + HC + oc + 1)
                    tensor.matmul(out=pstat_q[:], lhsT=ones128[:], rhs=sq_f[:],
                                  start=(oc == 0), stop=(oc == KC - 1)).then_inc(s_pe, 1)
                tensor.wait_ge(s_act, (t + 1) * ACT_T)
                if t >= 1:
                    tensor.wait_ge(s_ev, t * EV_T)
                tensor.matmul(out=pbc[:, 0:TT], lhsT=ones1[:], rhs=stat_m[:],
                              start=True, stop=True)
                tensor.matmul(out=pbc[:, TT:2 * TT], lhsT=ones1[:], rhs=stat_r[:],
                              start=True, stop=True).then_inc(s_pe, 1)

    return nc


class Sched:

    def __init__(self):
        self.steps = []
        self.count = {}
        self.engine_inc = {}

    def step(self, engine, deps, emit, inc=1):
        sid = len(self.steps)
        for d in deps:
            assert d is not None and d < sid, (d, sid)
        prev = self.engine_inc.setdefault(engine, inc)
        assert prev == inc, f"mixed inc on engine {engine}"
        seq = self.count.get(engine, 0) + 1
        self.count[engine] = seq
        self.steps.append((engine, list(deps), emit, inc, seq))
        return sid

    def emit_engine(self, engine_name, eng, sems):
        sem = sems[engine_name]
        for (e, deps, emit, inc, seq) in self.steps:
            if e != engine_name:
                continue
            for d in deps:
                de, _, _, dinc, dseq = self.steps[d]
                eng.wait_ge(sems[de], dseq * dinc)
            inst = emit(eng)
            assert inst is not None, "emit must return the last instruction"
            inst.then_inc(sem, inc)

    def final_value(self, engine_name):
        return self.count.get(engine_name, 0) * self.engine_inc.get(engine_name, 1)



def build_l0_ffn(NT=4096, H=3072, TT=512):
    nc = bass.Bass("TRN2", target_bir_lowering=False, detect_race_conditions=False)
    xT = nc.dram_tensor("xT", [D, NT], F32, kind="ExternalInput").ap()
    wgT = nc.dram_tensor("wgT", [D, H], I32, kind="ExternalInput").ap()
    wuT = nc.dram_tensor("wuT", [D, H], I32, kind="ExternalInput").ap()
    wdT = nc.dram_tensor("wdT", [H, D], I32, kind="ExternalInput").ap()
    yT = nc.dram_tensor("yT", [D, NT], F32, kind="ExternalOutput").ap()

    S = 1.0 / 127.0
    HC = H // 128
    NTILE = ceil_div(NT, TT)
    assert NT % TT == 0

    PE_T = 2 * HC + KC + 1 + KC + 1
    ACT_T = HC + KC + 1
    EV_T = HC + KC + 1 + 1

    with ExitStack() as ctx:
        sb = lambda name, shape, dt=F32: ctx.enter_context(nc.sbuf_tensor(name, shape, dt))
        ps = lambda name, shape, dt=F32: ctx.enter_context(nc.psum_tensor(name, shape, dt))
        sem = lambda name: ctx.enter_context(nc.semaphore(name))

        wg_bf = sb("wg_bf", [128, KC, H], BF16)
        wu_bf = sb("wu_bf", [128, KC, H], BF16)
        wd_bf = sb("wd_bf", [128, HC, D], BF16)
        ones128 = sb("ones128", [128, 1])
        ones1 = sb("ones1", [1, 128])
        x_f = sb("x_f", [128, KC, TT])
        x_bf = sb("x_bf", [128, KC, TT], BF16)
        h_bf = sb("h_bf", [128, HC, TT], BF16)
        sg_bf = sb("sg_bf", [128, TT], BF16)
        sq_f = sb("sq_f", [128, TT])
        ev_tmp = sb("ev_tmp", [128, TT])
        stat_m = sb("stat_m", [1, TT])
        stat_r = sb("stat_r", [1, TT])

        pmm = [ps(f"pmm{i}", [128, TT]) for i in range(4)]
        pstat_m = ps("pstat_m", [1, TT])
        pstat_q = ps("pstat_q", [1, TT])
        pbc = ps("pbc", [128, 2 * TT])

        s_wt = sem("s_wt"); s_ldx = sem("s_ldx"); s_xbf = sem("s_xbf")
        s_pe = sem("s_pe"); s_act = sem("s_act"); s_ev = sem("s_ev"); s_st = sem("s_st")

        block = ctx.enter_context(nc.Block())

        def store_tile(sync, t):
            sync.wait_ge(s_ev, (t + 1) * EV_T)
            sync.dma_start(
                out=yT[:, t * TT:(t + 1) * TT].rearrange("(kc p) t -> p kc t", p=128),
                in_=x_f[:],
            ).then_inc(s_st, 16)

        @block.sync
        def _(sync):
            for t in range(NTILE):
                t0 = t * TT
                if t >= 1:
                    store_tile(sync, t - 1)
                    sync.wait_ge(s_st, 16 * t)
                sync.dma_start(
                    out=x_f[:],
                    in_=xT[:, t0:t0 + TT].rearrange("(kc p) t -> p kc t", p=128),
                ).then_inc(s_ldx, 16)
            store_tile(sync, NTILE - 1)
            sync.wait_ge(s_st, 16 * NTILE)

        @block.gpsimd
        def _(gpsimd):
            gpsimd.dma_start(out=wg_bf[:],
                             in_=wgT.rearrange("(kc p) h -> p kc h", p=128)).then_inc(s_wt, 16)
            gpsimd.dma_start(out=wu_bf[:],
                             in_=wuT.rearrange("(kc p) h -> p kc h", p=128)).then_inc(s_wt, 16)
            gpsimd.dma_start(out=wd_bf[:],
                             in_=wdT.rearrange("(hc p) d -> p hc d", p=128)).then_inc(s_wt, 16)

        N_WT = 3

        @block.vector
        def _(vector):
            vector.memset(ones128[:], 1.0)
            vector.memset(ones1[:], 1.0).then_inc(s_xbf, 1)
            for t in range(NTILE):
                vector.wait_ge(s_ldx, 16 * (t + 1))
                if t >= 1:
                    vector.wait_ge(s_pe, (t - 1) * PE_T + 2 * HC)
                vector.tensor_copy(out=x_bf[:], in_=x_f[:]).then_inc(s_xbf, 1)
                for hc in range(HC):
                    up_bank = pmm[2 * (hc % 2) + 1]
                    vector.wait_ge(s_pe, t * PE_T + 2 * hc + 2)
                    vector.wait_ge(s_act, t * ACT_T + hc + 1)
                    vector.scalar_tensor_tensor(out=h_bf[:, hc, :], in0=up_bank[:],
                                                scalar=S, in1=sg_bf[:],
                                                op0=ALU.mult, op1=ALU.mult).then_inc(s_ev, 1)
                for oc in range(KC):
                    vector.wait_ge(s_pe, t * PE_T + 2 * HC + oc + 1)
                    bank = pmm[oc % 4]
                    vector.scalar_tensor_tensor(out=x_f[:, oc, :], in0=bank[:],
                                                scalar=S, in1=x_f[:, oc, :],
                                                op0=ALU.mult, op1=ALU.add).then_inc(s_ev, 1)
                vector.wait_ge(s_pe, t * PE_T + 2 * HC + KC + 1 + KC)
                vector.tensor_scalar(out=stat_m[:], in0=pstat_m[:],
                                     scalar1=1.0 / D, scalar2=None, op0=ALU.mult)
                vector.tensor_scalar(out=stat_r[:], in0=pstat_q[:],
                                     scalar1=1.0 / D, scalar2=None, op0=ALU.mult)
                vector.tensor_tensor(out=ev_tmp[0:1, :], in0=stat_m[:], in1=stat_m[:], op=ALU.mult)
                vector.tensor_sub(out=stat_r[:], in0=stat_r[:], in1=ev_tmp[0:1, :])
                vector.tensor_scalar(out=stat_r[:], in0=stat_r[:],
                                     scalar1=float(EPS), scalar2=None, op0=ALU.add)
                vector.reciprocal(out=stat_r[:], in_=stat_r[:]).then_inc(s_ev, 1)
                vector.wait_ge(s_pe, (t + 1) * PE_T)
                for oc in range(KC):
                    vector.tensor_sub(out=x_f[:, oc, :], in0=x_f[:, oc, :], in1=pbc[:, 0:TT])
                    inst = vector.tensor_mul(out=x_f[:, oc, :], in0=x_f[:, oc, :], in1=pbc[:, TT:2 * TT])
                inst.then_inc(s_ev, 1)

        @block.scalar
        def _(scalar):
            for t in range(NTILE):
                for hc in range(HC):
                    scalar.wait_ge(s_pe, t * PE_T + 2 * hc + 1)
                    if hc >= 1:
                        scalar.wait_ge(s_ev, t * EV_T + hc)
                    gate_bank = pmm[2 * (hc % 2)]
                    scalar.activation(out=sg_bf[:], in_=gate_bank[:], func=AF.Silu,
                                      scale=S).then_inc(s_act, 1)
                for oc in range(KC):
                    scalar.wait_ge(s_ev, t * EV_T + HC + oc + 1)
                    if oc >= 1:
                        scalar.wait_ge(s_pe, t * PE_T + 2 * HC + KC + 1 + oc)
                    scalar.activation(out=sq_f[:], in_=x_f[:, oc, :],
                                      func=AF.Square).then_inc(s_act, 1)
                scalar.wait_ge(s_ev, t * EV_T + HC + KC + 1)
                scalar.activation(out=stat_r[:], in_=stat_r[:], func=AF.Sqrt).then_inc(s_act, 1)

        @block.tensor
        def _(tensor):
            tensor.wait_ge(s_wt, 16 * N_WT)
            for t in range(NTILE):
                tensor.wait_ge(s_xbf, t + 2)
                for hc in range(HC):
                    gb = pmm[2 * (hc % 2)]
                    ub = pmm[2 * (hc % 2) + 1]
                    if hc >= 2:
                        tensor.wait_ge(s_act, t * ACT_T + hc - 1)
                        tensor.wait_ge(s_ev, t * EV_T + hc - 1)
                    elif t >= 1:
                        tensor.wait_ge(s_ev, t * EV_T)
                    inst = None
                    for kc in range(KC):
                        inst = tensor.matmul(out=gb[:], lhsT=wg_bf[:, kc, hc * 128:(hc + 1) * 128],
                                             rhs=x_bf[:, kc, :],
                                             start=(kc == 0), stop=(kc == KC - 1))
                    inst.then_inc(s_pe, 1)
                    inst = None
                    for kc in range(KC):
                        inst = tensor.matmul(out=ub[:], lhsT=wu_bf[:, kc, hc * 128:(hc + 1) * 128],
                                             rhs=x_bf[:, kc, :],
                                             start=(kc == 0), stop=(kc == KC - 1))
                    inst.then_inc(s_pe, 1)
                tensor.wait_ge(s_ev, t * EV_T + HC)
                for oc in range(KC):
                    bank = pmm[oc % 4]
                    if oc >= 4:
                        tensor.wait_ge(s_ev, t * EV_T + HC + oc - 3)
                    inst = None
                    for hc in range(HC):
                        inst = tensor.matmul(out=bank[:], lhsT=wd_bf[:, hc, oc * 128:(oc + 1) * 128],
                                             rhs=h_bf[:, hc, :],
                                             start=(hc == 0), stop=(hc == HC - 1))
                    inst.then_inc(s_pe, 1)
                inst = None
                for oc in range(KC):
                    tensor.wait_ge(s_ev, t * EV_T + HC + oc + 1)
                    inst = tensor.matmul(out=pstat_m[:], lhsT=ones128[:], rhs=x_f[:, oc, :],
                                         start=(oc == 0), stop=(oc == KC - 1))
                inst.then_inc(s_pe, 1)
                for oc in range(KC):
                    tensor.wait_ge(s_act, t * ACT_T + HC + oc + 1)
                    tensor.matmul(out=pstat_q[:], lhsT=ones128[:], rhs=sq_f[:],
                                  start=(oc == 0), stop=(oc == KC - 1)).then_inc(s_pe, 1)
                tensor.wait_ge(s_act, (t + 1) * ACT_T)
                if t >= 1:
                    tensor.wait_ge(s_ev, t * EV_T)
                tensor.matmul(out=pbc[:, 0:TT], lhsT=ones1[:], rhs=stat_m[:],
                              start=True, stop=True)
                tensor.matmul(out=pbc[:, TT:2 * TT], lhsT=ones1[:], rhs=stat_r[:],
                              start=True, stop=True).then_inc(s_pe, 1)

    return nc



def build_l1_attn(T=8192, G=512, L=128):
    nc = bass.Bass("TRN2", target_bir_lowering=False, detect_race_conditions=False)
    xT = nc.dram_tensor("xT", [D, T], F32, kind="ExternalInput").ap()
    wq = nc.dram_tensor("wq", [D, 512], F32, kind="ExternalInput").ap()
    wk = nc.dram_tensor("wk", [D, 512], F32, kind="ExternalInput").ap()
    wv = nc.dram_tensor("wv", [D, 512], F32, kind="ExternalInput").ap()
    wgk = nc.dram_tensor("wgk", [D, 512], F32, kind="ExternalInput").ap()
    wgc = nc.dram_tensor("wgc", [D, 512], F32, kind="ExternalInput").ap()
    bgk = nc.dram_tensor("bgk", [512], F32, kind="ExternalInput").ap()
    bgc = nc.dram_tensor("bgc", [512], F32, kind="ExternalInput").ap()
    oT = nc.dram_tensor("oT", [512, T], F32, kind="ExternalOutput").ap()

    NG = T // G
    NCH = G // L
    NP = 4

    incl = np.tril(np.ones((L, L), np.float32)).T
    strict = (incl - np.eye(L)).astype(np.float32)
    mask4_np = np.concatenate([incl, strict], axis=1)
    mask4_d = nc.inline_tensor(mask4_np.astype(np.float32), "mask4").ap()
    idb_d = nc.inline_tensor(np.eye(L, dtype=ml_dtypes.bfloat16), "idb").ap()
    idf_d = nc.inline_tensor(np.eye(L, dtype=np.float32), "idf").ap()
    ones1_np = np.ones((1, 128), np.float32)
    ones1_d = nc.inline_tensor(ones1_np, "ones1").ap()
    mA_np = np.zeros((1, 128), np.float32); mA_np[0, :64] = 1.0
    mB_np = np.zeros((1, 128), np.float32); mB_np[0, 64:] = 1.0
    mA_d = nc.inline_tensor(mA_np, "maskA").ap()
    mB_d = nc.inline_tensor(mB_np, "maskB").ap()

    with ExitStack() as ctx:
        sb = lambda name, shape, dt=F32: ctx.enter_context(nc.sbuf_tensor(name, shape, dt))
        ps = lambda name, shape, dt=F32: ctx.enter_context(nc.psum_tensor(name, shape, dt))
        sem = lambda name: ctx.enter_context(nc.semaphore(name))

        wq_b = sb("wq_b", [128, KC, 512], BF16)
        wk_b = sb("wk_b", [128, KC, 512], BF16)
        wv_b = sb("wv_b", [128, KC, 512], BF16)
        wgk_b = sb("wgk_b", [128, KC, 512], BF16)
        wgc_b = sb("wgc_b", [128, KC, 512], BF16)
        bgk_s = sb("bgk_s", [128, 4])
        bgc_s = sb("bgc_s", [128, 4])
        mask4_s = sb("mask4_s", [128, 256])
        idb_s = sb("idb_s", [128, 128], BF16)
        idf_s = sb("idf_s", [128, 128])
        ones1_s = sb("ones1_s", [1, 128])
        mA_s = sb("mA_s", [1, 128])
        mB_s = sb("mB_s", [1, 128])
        x_bf = sb("x_bf", [128, KC, G], BF16)
        Q_f = sb("Q_f", [128, NP, G])
        K_f = sb("K_f", [128, NP, G])
        GA_f = sb("GA_f", [128, NP, G])
        GC_f = sb("GC_f", [128, NP, G])
        V_b = sb("V_b", [128, NCH, 512], BF16)
        beta = sb("beta", [128, NP, L])
        gamma = sb("gamma", [128, NP, L])
        rec_f = sb("rec_f", [128, NP, L])
        rec2_f = sb("rec2_f", [128, NP, L])
        qt_b = sb("qt_b", [128, NP, L], BF16)
        kh_b = sb("kh_b", [128, NP, L], BF16)
        kt_b = sb("kt_b", [128, NP, L], BF16)
        qh_b = sb("qh_b", [128, NP, L], BF16)
        st_b = sb("st_b", [128, L], BF16)
        s_f = sb("s_f", [128, L])
        Pm_b = sb("Pm_b", [128, 512], BF16)
        P3m_b = sb("P3m_b", [128, 256], BF16)
        ktm_b = sb("ktm_b", [128, 128], BF16)
        qhtm_b = sb("qhtm_b", [128, 128], BF16)
        rn_b = sb("rn_b", [128, 128], BF16)
        rtm_b = sb("rtm_b", [128, 128], BF16)
        Ktl_b = sb("Ktl_b", [128, 128], BF16)
        Qtl_b = sb("Qtl_b", [128, 128], BF16)
        brow = sb("brow", [1, 128])
        nopt = sb("nopt", [1, 4])
        grow = sb("grow", [1, 128])
        S_f = sb("S_f", [128, NP, 64])
        C_f = sb("C_f", [128, NP, 64])
        Gn_f = sb("Gn_f", [128, NP, 64])
        Sb = sb("Sb", [128, NP, 64], BF16)
        Cb = sb("Cb", [128, NP, 64], BF16)
        Gb = sb("Gb", [128, NP, 64], BF16)
        Ssc = sb("Ssc", [128, 64])
        o_acc = sb("o_acc", [128, NP, G])

        pPr0 = ps("pPr0", [128, 512])
        pPr1 = ps("pPr1", [128, 512])
        pA = ps("pA", [128, 256])
        pB2 = ps("pB2", [128, 256])
        pS = ps("pS", [128, 384])
        pT = ps("pT", [128, 512], BF16)
        pB = ps("pB", [128, 320])
        pU = ps("pU", [128, 192])

        sems = {n: sem("sm_" + n) for n in ("sync", "gpsimd", "tensor", "vector", "scalar")}
        sd = Sched()
        import os as _os
        LVL = int(_os.environ.get("L1A_LVL", "99"))

        def gstep(engine, deps, emit, inc=1, lvl=0):
            if lvl > LVL and engine in ("tensor", "vector", "scalar"):
                if engine == "tensor":
                    emit = lambda e: e.matmul(out=pU[0:1, 0:1], lhsT=ones1_s[0:1, 0:1],
                                              rhs=ones1_s[0:1, 0:1], start=True, stop=True)
                elif engine == "scalar":
                    emit = lambda e: e.copy(out=nopt[0:1, 0:1], in_=nopt[0:1, 0:1])
                else:
                    emit = lambda e: e.engine_nop()
            return sd.step(engine, deps, emit, inc)

        for dst, src_ap in ((mask4_s, mask4_d), (idb_s, idb_d), (idf_s, idf_d),
                            (ones1_s, ones1_d), (mA_s, mA_d), (mB_s, mB_d)):
            ld_c = sd.step("sync", [], (lambda dst=dst, src_ap=src_ap:
                                        (lambda e: e.dma_start(out=dst[:], in_=src_ap)))(), inc=16)
        ld_w = {}
        for nm, wd, wt in (("q", wq, wq_b), ("k", wk, wk_b), ("v", wv, wv_b),
                           ("gk", wgk, wgk_b), ("gc", wgc, wgc_b)):
            ld_w[nm] = sd.step("gpsimd", [], (lambda wd=wd, wt=wt: (lambda e: e.dma_start(
                out=wt[:], in_=wd.rearrange("(kc p) n -> p kc n", p=128))))(), inc=16)
        ld_bg = []
        for c in range(4):
            ld_bg.append(sd.step("gpsimd", [], (lambda c=c: (lambda e: e.dma_start(
                out=bgk_s[:, c:c + 1], in_=bgk[c * 128:(c + 1) * 128].unsqueeze(1))))(), inc=16))
            ld_bg.append(sd.step("gpsimd", [], (lambda c=c: (lambda e: e.dma_start(
                out=bgc_s[:, c:c + 1], in_=bgc[c * 128:(c + 1) * 128].unsqueeze(1))))(), inc=16))
        init_st = sd.step("vector", [], lambda e: (
            e.memset(S_f[:], 0.0), e.memset(C_f[:], 0.0), e.memset(Gn_f[:], 0.0))[-1])

        CONSTS = ld_c

        prev = {}
        for g in range(NG):
            g0 = g * G
            xdeps = [prev["proj_mm_last"]] if "proj_mm_last" in prev else []
            ld_x = sd.step("gpsimd", xdeps, (lambda g0=g0: (lambda e: e.dma_start(
                out=x_bf[:], in_=xT[:, g0:g0 + G].rearrange("(kc p) t -> p kc t", p=128))))(), inc=16)

            projs = [("q", wq_b, Q_f, "copy"), ("k", wk_b, K_f, "copy"),
                     ("gk", wgk_b, GA_f, "sig_k"), ("gc", wgc_b, GC_f, "sig_c")]
            pi = 0
            evac_steps = []
            for nm, wtile, dst, kind in projs:
                for pc in range(NP):
                    bank = pPr0 if pi % 2 == 0 else pPr1
                    deps = [ld_x, ld_w[nm]]
                    if len(evac_steps) >= 2:
                        deps.append(evac_steps[-2])
                    if g > 0 and pi == 0:
                        deps.append(prev["last_chunk_tensor"])

                    def mk_mm(bank=bank, wtile=wtile, pc=pc):
                        def f(e):
                            inst = None
                            for kc in range(KC):
                                inst = e.matmul(out=bank[:], lhsT=wtile[:, kc, pc * 128:(pc + 1) * 128],
                                                rhs=x_bf[:, kc, :], start=(kc == 0), stop=(kc == KC - 1))
                            return inst
                        return f
                    mm = sd.step("tensor", deps, mk_mm())
                    if kind == "copy":
                        evdeps = [mm]
                        if g > 0 and pi < 2:
                            evdeps.append(prev["last_chunk_tensor"])

                        def mk_ev(bank=bank, dst=dst, pc=pc):
                            return lambda e: e.tensor_copy(out=dst[:, pc, :], in_=bank[:])
                        ev = sd.step("vector", evdeps, mk_ev())
                    else:
                        bias = bgk_s if kind == "sig_k" else bgc_s
                        evdeps = [mm] + ld_bg[-1:]
                        if g > 0:
                            evdeps.append(prev["last_chunk_vb1"])

                        def mk_ev(bank=bank, dst=dst, pc=pc, bias=bias):
                            return lambda e: e.activation(out=dst[:, pc, :], in_=bank[:],
                                                          func=AF.Sigmoid, bias=bias[:, pc:pc + 1],
                                                          scale=1.0)
                        ev = sd.step("scalar", evdeps, mk_ev())
                    evac_steps.append(ev)
                    pi += 1
            for sub in range(NCH):
                bank = pPr0 if pi % 2 == 0 else pPr1
                deps = [ld_x, ld_w["v"]]
                if len(evac_steps) >= 2:
                    deps.append(evac_steps[-2])

                def mk_vmm(bank=bank, sub=sub):
                    def f(e):
                        inst = None
                        for kc in range(KC):
                            inst = e.matmul(out=bank[:], lhsT=x_bf[:, kc, sub * 128:(sub + 1) * 128],
                                            rhs=wv_b[:, kc, :], start=(kc == 0), stop=(kc == KC - 1))
                        return inst
                    return f
                mm = sd.step("tensor", deps, mk_vmm())
                evdeps = [mm]
                if g > 0 and sub == 0:
                    evdeps.append(prev["last_chunk_tensor"])

                def mk_vev(bank=bank, sub=sub):
                    return lambda e: e.tensor_copy(out=V_b[:, sub, :], in_=bank[:])
                ev = sd.step("vector", evdeps, mk_vev())
                evac_steps.append(ev)
                pi += 1
            prev["proj_mm_last"] = mm
            proj_all = list(evac_steps)

            for ch in range(NCH):
                c0 = ch * L
                vb1deps = [proj_all[8 + 0], proj_all[8 + 1], proj_all[8 + 2], proj_all[8 + 3],
                           proj_all[12 + 0], proj_all[12 + 1], proj_all[12 + 2], proj_all[12 + 3]]
                if "t_rows_last" in prev:
                    vb1deps.append(prev["t_rows_last"])

                def mk_vb1(c0=c0):
                    def f(e):
                        inst = None
                        for pr in range(NP):
                            inst = e.tensor_tensor_scan(
                                out=beta[:, pr, :], data0=GA_f[:, pr, c0:c0 + L],
                                data1=GA_f[:, pr, c0:c0 + L], initial=1.0,
                                op0=ALU.mult, op1=ALU.bypass)
                            inst = e.tensor_tensor_scan(
                                out=gamma[:, pr, :], data0=GC_f[:, pr, c0:c0 + L],
                                data1=GC_f[:, pr, c0:c0 + L], initial=1.0,
                                op0=ALU.mult, op1=ALU.bypass)
                        return inst
                    return f
                vb1 = gstep("vector", vb1deps, mk_vb1(), lvl=2)
                prev["last_chunk_vb1"] = vb1

                vb2deps = [vb1, proj_all[0], proj_all[1], proj_all[2], proj_all[3],
                           proj_all[4], proj_all[5], proj_all[6], proj_all[7]]
                if "t_upd_last" in prev:
                    vb2deps.append(prev["t_upd_last"])

                vb2a = gstep("vector", vb2deps, (lambda c0=c0: (lambda e: (
                    e.tensor_mul(out=qt_b[:], in0=beta[:], in1=Q_f[:, :, c0:c0 + L]),
                    e.reciprocal(out=rec_f[:], in_=beta[:]))[-1]))(), lvl=2)
                vb2b = gstep("vector", [vb2a], (lambda c0=c0: (lambda e: (
                    e.tensor_mul(out=kh_b[:], in0=K_f[:, :, c0:c0 + L], in1=rec_f[:]),
                    e.tensor_mul(out=kt_b[:], in0=gamma[:], in1=K_f[:, :, c0:c0 + L]),
                    e.reciprocal(out=rec2_f[:], in_=gamma[:]))[-1]))(), lvl=2)
                vb2 = gstep("vector", [vb2b], (lambda c0=c0: (lambda e: e.tensor_mul(
                    out=qh_b[:], in0=Q_f[:, :, c0:c0 + L], in1=rec2_f[:])))(), lvl=2)

                vb3deps = []
                if "t_upd_last" in prev:
                    vb3deps.append(prev["t_upd_last"])

                def mk_vb3():
                    def f(e):
                        e.tensor_copy(out=Sb[:], in_=S_f[:])
                        e.tensor_copy(out=Cb[:], in_=C_f[:])
                        return e.tensor_copy(out=Gb[:], in_=Gn_f[:])
                    return f
                vb3 = gstep("vector", vb3deps, mk_vb3(), lvl=2)

                for pr in range(NP):
                    A = slice(0, 64)
                    B = slice(64, 128)
                    dvA = slice(pr * 128, pr * 128 + 64)
                    dvB = slice(pr * 128 + 64, pr * 128 + 128)

                    tPdeps = [vb2, CONSTS]
                    if "v_P3m" in prev:
                        tPdeps.append(prev["v_P3m"])

                    def mk_tP(pr=pr):
                        def f(e):
                            e.matmul(out=pA[:, 0:128], lhsT=kh_b[A, pr, :], rhs=qt_b[A, pr, :],
                                     start=True, stop=True, tile_position=(0, 0))
                            e.matmul(out=pB2[:, 0:128], lhsT=kh_b[B, pr, :], rhs=qt_b[B, pr, :],
                                     start=True, stop=True, tile_position=(64, 0))
                            e.matmul(out=pA[:, 128:256], lhsT=qh_b[A, pr, :], rhs=kt_b[A, pr, :],
                                     start=True, stop=True, tile_position=(0, 0))
                            return e.matmul(out=pB2[:, 128:256], lhsT=qh_b[B, pr, :], rhs=kt_b[B, pr, :],
                                            start=True, stop=True, tile_position=(64, 0))
                        return f
                    t_P = gstep("tensor", tPdeps, mk_tP(), lvl=3)

                    vPmdeps = [t_P]
                    if "t_o" in prev:
                        vPmdeps.append(prev["t_o"])
                    v_Pm = gstep("vector", vPmdeps, lambda e: (
                        e.tensor_mul(out=Pm_b[:, 0:256], in0=pA[:], in1=mask4_s[:]),
                        e.tensor_mul(out=Pm_b[:, 256:512], in0=pB2[:], in1=mask4_s[:]))[-1], lvl=3)

                    trKdeps = [vb2]
                    if "v_qhtm" in prev:
                        trKdeps.append(prev["v_qhtm"])

                    def mk_trK(pr=pr):
                        return lambda e: e.transpose(out=pT[:, 0:128], in_=kh_b[:, pr, :], identity=idb_s[:])
                    t_trK = gstep("tensor", trKdeps, mk_trK(), lvl=4)
                    v_ktm = gstep("vector", [t_trK], lambda e: e.tensor_copy(out=ktm_b[:], in_=pT[:, 0:128]), lvl=4)
                    prev["v_ktm"] = v_ktm

                    tsdeps = [v_ktm, v_Pm, vb3]
                    if "v_s" in prev:
                        tsdeps.append(prev["v_s"])
                    if "v_oev" in prev:
                        tsdeps.append(prev["v_oev"])

                    def mk_ts(pr=pr):
                        def f(e):
                            e.matmul(out=pS[0:64, 0:128], lhsT=ktm_b[:, 0:64], rhs=Pm_b[:, 0:128],
                                     start=True, stop=False, tile_position=(0, 0))
                            e.matmul(out=pS[0:64, 0:128], lhsT=Sb[A, pr, :], rhs=qt_b[A, pr, :],
                                     start=False, stop=True, tile_position=(0, 0))
                            e.matmul(out=pS[64:128, 0:128], lhsT=ktm_b[:, 64:128], rhs=Pm_b[:, 256:384],
                                     start=True, stop=False, tile_position=(0, 64))
                            return e.matmul(out=pS[64:128, 0:128], lhsT=Sb[B, pr, :], rhs=qt_b[B, pr, :],
                                            start=False, stop=True, tile_position=(64, 64))
                        return f
                    t_s = gstep("tensor", tsdeps, mk_ts(), lvl=4)
                    prev["t_s_prev"] = t_s

                    vsdeps = [t_s]
                    if "t_o" in prev:
                        vsdeps.append(prev["t_o"])

                    v_sa = gstep("vector", vsdeps, (lambda pr=pr: (lambda e: e.tensor_mul(
                        out=s_f[:], in0=beta[:, pr, :], in1=pS[:, 0:128])))(), lvl=4)
                    v_s = gstep("vector", [v_sa], (lambda pr=pr: (lambda e: e.tensor_mul(
                        out=st_b[:], in0=gamma[:, pr, :], in1=s_f[:])))(), lvl=4)
                    prev["v_s"] = v_s

                    trdeps = [v_Pm, vb3, proj_all[-1], v_s]
                    if "v_rneg" in prev:
                        trdeps.append(prev["v_rneg"])

                    def mk_tr(pr=pr, ch=ch, dvA=dvA, dvB=dvB):
                        def f(e):
                            e.matmul(out=pS[0:64, 128:256], lhsT=V_b[:, ch, dvA], rhs=Pm_b[:, 128:256],
                                     start=True, stop=False, tile_position=(0, 0))
                            e.matmul(out=pS[0:64, 128:256], lhsT=Cb[A, pr, :], rhs=kt_b[A, pr, :],
                                     start=False, stop=True, tile_position=(0, 0))
                            e.matmul(out=pS[64:128, 128:256], lhsT=V_b[:, ch, dvB], rhs=Pm_b[:, 384:512],
                                     start=True, stop=False, tile_position=(0, 64))
                            return e.matmul(out=pS[64:128, 128:256], lhsT=Cb[B, pr, :], rhs=kt_b[B, pr, :],
                                            start=False, stop=True, tile_position=(64, 64))
                        return f
                    t_r = gstep("tensor", trdeps, mk_tr(), lvl=5)

                    vrndeps = [t_r]
                    if "t_trR" in prev:
                        vrndeps.append(prev["t_trR"])
                    v_rneg = gstep("vector", vrndeps,
                                     lambda e: e.tensor_scalar(out=rn_b[:], in0=pS[:, 128:256],
                                                               scalar1=-1.0, scalar2=None, op0=ALU.mult), lvl=5)
                    prev["v_rneg"] = v_rneg

                    trRdeps = [v_rneg]
                    if "v_rtm" in prev:
                        trRdeps.append(prev["v_rtm"])
                    t_trR = gstep("tensor", trRdeps,
                                    lambda e: e.transpose(out=pT[:, 256:384], in_=rn_b[:], identity=idb_s[:]), lvl=5)
                    prev["t_trR"] = t_trR
                    v_rtm = gstep("vector", [t_trR], lambda e: e.tensor_copy(out=rtm_b[:], in_=pT[:, 256:384]), lvl=5)
                    prev["v_rtm"] = v_rtm

                    trQdeps = [vb2, v_rtm]

                    def mk_trQ(pr=pr):
                        return lambda e: e.transpose(out=pT[:, 128:256], in_=qh_b[:, pr, :], identity=idb_s[:])
                    t_trQ = gstep("tensor", trQdeps, mk_trQ(), lvl=5)
                    v_qhtm = gstep("vector", [t_trQ], lambda e: e.tensor_copy(out=qhtm_b[:], in_=pT[:, 128:256]), lvl=5)
                    prev["v_qhtm"] = v_qhtm

                    tP3deps = [v_s, v_Pm]

                    def mk_tP3(pr=pr):
                        def f(e):
                            e.matmul(out=pA[:, 0:128], lhsT=qh_b[A, pr, :], rhs=st_b[A, :],
                                     start=True, stop=True, tile_position=(0, 0))
                            return e.matmul(out=pB2[:, 0:128], lhsT=qh_b[B, pr, :], rhs=st_b[B, :],
                                            start=True, stop=True, tile_position=(64, 0))
                        return f
                    t_P3 = gstep("tensor", tP3deps, mk_tP3(), lvl=6)

                    vP3mdeps = [t_P3]
                    if "t_o" in prev:
                        vP3mdeps.append(prev["t_o"])
                    v_P3m = gstep("vector", vP3mdeps, lambda e: (
                        e.tensor_mul(out=P3m_b[:, 0:128], in0=pA[:, 0:128], in1=mask4_s[:, 0:128]),
                        e.tensor_mul(out=P3m_b[:, 128:256], in0=pB2[:, 0:128], in1=mask4_s[:, 0:128]))[-1], lvl=6)
                    prev["v_P3m"] = v_P3m

                    todeps = [v_P3m, v_rtm, vb3]
                    if "v_oev" in prev:
                        todeps.append(prev["v_oev"])

                    def mk_to(pr=pr, ch=ch, dvA=dvA, dvB=dvB):
                        def f(e):
                            e.matmul(out=pS[0:64, 256:384], lhsT=V_b[:, ch, dvA], rhs=P3m_b[:, 0:128],
                                     start=True, stop=False, tile_position=(0, 0))
                            e.matmul(out=pS[0:64, 256:384], lhsT=Cb[A, pr, :], rhs=st_b[A, :],
                                     start=False, stop=False, tile_position=(0, 0))
                            e.matmul(out=pS[0:64, 256:384], lhsT=rtm_b[:, 0:64], rhs=Pm_b[:, 0:128],
                                     start=False, stop=False, tile_position=(0, 0))
                            e.matmul(out=pS[0:64, 256:384], lhsT=Gb[A, pr, :], rhs=qt_b[A, pr, :],
                                     start=False, stop=True, tile_position=(0, 0))
                            e.matmul(out=pS[64:128, 256:384], lhsT=V_b[:, ch, dvB], rhs=P3m_b[:, 128:256],
                                     start=True, stop=False, tile_position=(0, 64))
                            e.matmul(out=pS[64:128, 256:384], lhsT=Cb[B, pr, :], rhs=st_b[B, :],
                                     start=False, stop=False, tile_position=(64, 64))
                            e.matmul(out=pS[64:128, 256:384], lhsT=rtm_b[:, 64:128], rhs=Pm_b[:, 256:384],
                                     start=False, stop=False, tile_position=(0, 64))
                            return e.matmul(out=pS[64:128, 256:384], lhsT=Gb[B, pr, :], rhs=qt_b[B, pr, :],
                                            start=False, stop=True, tile_position=(64, 64))
                        return f
                    t_o = gstep("tensor", todeps, mk_to(), lvl=6)
                    prev["t_o"] = t_o

                    voevdeps = [t_o]
                    if "store" in prev and ch == 0 and pr == 0:
                        voevdeps.append(prev["store"])

                    def mk_voev(pr=pr, c0=c0):
                        return lambda e: e.tensor_copy(out=o_acc[:, pr, c0:c0 + L], in_=pS[:, 256:384])
                    v_oev = gstep("vector", voevdeps, mk_voev(), lvl=6)
                    prev["v_oev"] = v_oev

                    trowdeps = [vb1, CONSTS]
                    if "v_rows" in prev:
                        trowdeps.append(prev["v_rows"])
                    if "v_ktl" in prev:
                        trowdeps.append(prev["v_ktl"])
                    if "v_ssc" in prev:
                        trowdeps.append(prev["v_ssc"])

                    def mk_trows(pr=pr):
                        def f(e):
                            e.transpose(out=pB[0:1, 0:128], in_=beta[:, pr, L - 1:L], identity=idf_s[:])
                            return e.transpose(out=pB[0:1, 128:256], in_=gamma[:, pr, L - 1:L], identity=idf_s[:])
                        return f
                    t_rows = gstep("tensor", trowdeps, mk_trows(), lvl=7)
                    prev["t_rows_last"] = t_rows
                    v_rows = gstep("vector", [t_rows], lambda e: (
                        e.tensor_copy(out=brow[:], in_=pB[0:1, 0:128]),
                        e.tensor_copy(out=grow[:], in_=pB[0:1, 128:256]))[-1], lvl=7)
                    prev["v_rows"] = v_rows

                    tbcdeps = [v_rows, CONSTS]
                    if "v_ktl" in prev:
                        tbcdeps.append(prev["v_ktl"])
                    if "v_ssc" in prev:
                        tbcdeps.append(prev["v_ssc"])

                    def mk_tbc():
                        def f(e):
                            e.matmul(out=pB[:, 0:128], lhsT=ones1_s[:], rhs=brow[:], start=True, stop=True)
                            e.matmul(out=pB[:, 128:256], lhsT=ones1_s[:], rhs=grow[:], start=True, stop=True)
                            e.matmul(out=pB[:, 256:320], lhsT=mA_s[:], rhs=brow[0:1, 0:64], start=True, stop=False)
                            return e.matmul(out=pB[:, 256:320], lhsT=mB_s[:], rhs=brow[0:1, 64:128], start=False, stop=True)
                        return f
                    t_bc = gstep("tensor", tbcdeps, mk_tbc(), lvl=7)

                    vktldeps = [t_bc, v_ktm, v_qhtm]
                    if "t_upd_prevpair" in prev:
                        vktldeps.append(prev["t_upd_prevpair"])
                    v_ktl = gstep("vector", vktldeps, lambda e: (
                        e.tensor_mul(out=Ktl_b[:], in0=ktm_b[:], in1=pB[:, 0:128]),
                        e.tensor_mul(out=Qtl_b[:], in0=qhtm_b[:], in1=pB[:, 128:256]))[-1], lvl=7)
                    prev["v_ktl"] = v_ktl

                    vsscdeps = [t_bc, vb1]

                    def mk_vssc(pr=pr):
                        def f(e):
                            e.tensor_scalar(out=Ssc[:], in0=S_f[:, pr, :],
                                            scalar1=beta[:, pr, L - 1:L], scalar2=None, op0=ALU.mult)
                            return e.tensor_mul(out=Ssc[:], in0=Ssc[:], in1=pB[:, 256:320])
                        return f
                    v_ssc = gstep("vector", vsscdeps, mk_vssc(), lvl=7)
                    prev["v_ssc"] = v_ssc

                    tupddeps = [v_ktl, v_rtm]
                    if "v_asm" in prev:
                        tupddeps.append(prev["v_asm"])

                    def mk_tupd(pr=pr, ch=ch, dvA=dvA, dvB=dvB):
                        def f(e):
                            e.matmul(out=pU[0:64, 0:64], lhsT=Ktl_b[:, 0:64], rhs=Ktl_b[:, 0:64],
                                     start=True, stop=True, tile_position=(0, 0))
                            e.matmul(out=pU[64:128, 0:64], lhsT=Ktl_b[:, 64:128], rhs=Ktl_b[:, 64:128],
                                     start=True, stop=True, tile_position=(0, 64))
                            e.matmul(out=pU[0:64, 64:128], lhsT=Qtl_b[:, 0:64], rhs=V_b[:, ch, dvA],
                                     start=True, stop=True, tile_position=(0, 0))
                            e.matmul(out=pU[64:128, 64:128], lhsT=Qtl_b[:, 64:128], rhs=V_b[:, ch, dvB],
                                     start=True, stop=True, tile_position=(0, 64))
                            e.matmul(out=pU[0:64, 128:192], lhsT=Ktl_b[:, 0:64], rhs=rtm_b[:, 0:64],
                                     start=True, stop=True, tile_position=(0, 0))
                            return e.matmul(out=pU[64:128, 128:192], lhsT=Ktl_b[:, 64:128], rhs=rtm_b[:, 64:128],
                                            start=True, stop=True, tile_position=(0, 64))
                        return f
                    t_upd = gstep("tensor", tupddeps, mk_tupd(), lvl=8)
                    prev["t_upd_prevpair"] = t_upd
                    prev["t_upd_last"] = t_upd

                    vasmdeps = [t_upd, v_ssc]

                    def mk_vasm(pr=pr):
                        def f(e):
                            e.tensor_add(out=S_f[:, pr, :], in0=Ssc[:], in1=pU[:, 0:64])
                            e.scalar_tensor_tensor(out=C_f[:, pr, :], in0=C_f[:, pr, :],
                                                   scalar=gamma[:, pr, L - 1:L], in1=pU[:, 64:128],
                                                   op0=ALU.mult, op1=ALU.add)
                            return e.scalar_tensor_tensor(out=Gn_f[:, pr, :], in0=Gn_f[:, pr, :],
                                                          scalar=beta[:, pr, L - 1:L], in1=pU[:, 128:192],
                                                          op0=ALU.mult, op1=ALU.add)
                        return f
                    v_asm = gstep("vector", vasmdeps, mk_vasm(), lvl=8)
                    prev["v_asm"] = v_asm

                prev["last_chunk_tensor"] = t_upd

            stdeps = [prev["v_oev"]]

            def mk_store(g0=g0):
                return lambda e: e.dma_start(
                    out=oT[:, g0:g0 + G].rearrange("(pr p) t -> p pr t", p=128),
                    in_=o_acc[:])
            prev["store"] = sd.step("sync", stdeps, mk_store(), inc=16)

        block = ctx.enter_context(nc.Block())

        @block.sync
        def _(sync):
            sd.emit_engine("sync", sync, sems)
            sync.wait_ge(sems["sync"], sd.final_value("sync"))

        @block.gpsimd
        def _(gpsimd):
            sd.emit_engine("gpsimd", gpsimd, sems)

        @block.vector
        def _(vector):
            sd.emit_engine("vector", vector, sems)

        @block.scalar
        def _(scalar):
            sd.emit_engine("scalar", scalar, sems)

        @block.tensor
        def _(tensor):
            sd.emit_engine("tensor", tensor, sems)

    return nc



def build_l1_attn2(NT=4096, TT=512):
    nc = bass.Bass("TRN2", target_bir_lowering=False, detect_race_conditions=False)
    oT = nc.dram_tensor("oT", [D, NT], F32, kind="ExternalInput").ap()
    xT = nc.dram_tensor("xT", [D, NT], F32, kind="ExternalInput").ap()
    wo = nc.dram_tensor("wo", [D, D], F32, kind="ExternalInput").ap()
    yT = nc.dram_tensor("yT", [D, NT], F32, kind="ExternalOutput").ap()

    NTILE = ceil_div(NT, TT)
    assert NT % TT == 0
    PE_T = KC + 1 + KC + 1
    ACT_T = KC + 1
    EV_T = KC + 1 + 1

    with ExitStack() as ctx:
        sb = lambda name, shape, dt=F32: ctx.enter_context(nc.sbuf_tensor(name, shape, dt))
        ps = lambda name, shape, dt=F32: ctx.enter_context(nc.psum_tensor(name, shape, dt))
        sem = lambda name: ctx.enter_context(nc.semaphore(name))

        wo_bf = sb("wo_bf", [128, KC, D], BF16)
        ones128 = sb("ones128", [128, 1])
        ones1 = sb("ones1", [1, 128])
        x_f = sb("x_f", [128, KC, TT])
        o_bf = sb("o_bf", [128, KC, TT], BF16)
        sq_f = sb("sq_f", [128, TT])
        ev_tmp = sb("ev_tmp", [128, TT])
        stat_m = sb("stat_m", [1, TT])
        stat_r = sb("stat_r", [1, TT])

        pmm = [ps(f"pmm{i}", [128, TT]) for i in range(4)]
        pstat_m = ps("pstat_m", [1, TT])
        pstat_q = ps("pstat_q", [1, TT])
        pbc = ps("pbc", [128, 2 * TT])

        s_wt = sem("s_wt"); s_ldx = sem("s_ldx"); s_ldo = sem("s_ldo")
        s_pe = sem("s_pe"); s_act = sem("s_act"); s_ev = sem("s_ev"); s_st = sem("s_st")

        block = ctx.enter_context(nc.Block())

        def store_tile(sync, t):
            sync.wait_ge(s_ev, (t + 1) * EV_T)
            sync.dma_start(
                out=yT[:, t * TT:(t + 1) * TT].rearrange("(kc p) t -> p kc t", p=128),
                in_=x_f[:],
            ).then_inc(s_st, 16)

        @block.sync
        def _(sync):
            for t in range(NTILE):
                t0 = t * TT
                if t >= 1:
                    store_tile(sync, t - 1)
                    sync.wait_ge(s_st, 16 * t)
                sync.dma_start(
                    out=x_f[:],
                    in_=xT[:, t0:t0 + TT].rearrange("(kc p) t -> p kc t", p=128),
                ).then_inc(s_ldx, 16)
            store_tile(sync, NTILE - 1)
            sync.wait_ge(s_st, 16 * NTILE)

        @block.gpsimd
        def _(gpsimd):
            gpsimd.dma_start(out=wo_bf[:],
                             in_=wo.rearrange("(kc p) n -> p kc n", p=128)).then_inc(s_wt, 16)
            for t in range(NTILE):
                t0 = t * TT
                if t >= 1:
                    gpsimd.wait_ge(s_pe, (t - 1) * PE_T + KC)
                gpsimd.dma_start(
                    out=o_bf[:],
                    in_=oT[:, t0:t0 + TT].rearrange("(kc p) t -> p kc t", p=128),
                ).then_inc(s_ldo, 16)

        @block.vector
        def _(vector):
            vector.memset(ones128[:], 1.0)
            vector.memset(ones1[:], 1.0)
            for t in range(NTILE):
                for oc in range(KC):
                    vector.wait_ge(s_pe, t * PE_T + oc + 1)
                    bank = pmm[oc % 4]
                    vector.tensor_add(out=x_f[:, oc, :], in0=bank[:],
                                      in1=x_f[:, oc, :]).then_inc(s_ev, 1)
                vector.wait_ge(s_pe, t * PE_T + KC + 1 + KC)
                vector.tensor_scalar(out=stat_m[:], in0=pstat_m[:],
                                     scalar1=1.0 / D, scalar2=None, op0=ALU.mult)
                vector.tensor_scalar(out=stat_r[:], in0=pstat_q[:],
                                     scalar1=1.0 / D, scalar2=None, op0=ALU.mult)
                vector.tensor_tensor(out=ev_tmp[0:1, :], in0=stat_m[:], in1=stat_m[:], op=ALU.mult)
                vector.tensor_sub(out=stat_r[:], in0=stat_r[:], in1=ev_tmp[0:1, :])
                vector.tensor_scalar(out=stat_r[:], in0=stat_r[:],
                                     scalar1=float(EPS), scalar2=None, op0=ALU.add)
                vector.reciprocal(out=stat_r[:], in_=stat_r[:]).then_inc(s_ev, 1)
                vector.wait_ge(s_pe, (t + 1) * PE_T)
                for oc in range(KC):
                    vector.tensor_sub(out=x_f[:, oc, :], in0=x_f[:, oc, :], in1=pbc[:, 0:TT])
                    inst = vector.tensor_mul(out=x_f[:, oc, :], in0=x_f[:, oc, :], in1=pbc[:, TT:2 * TT])
                inst.then_inc(s_ev, 1)

        @block.scalar
        def _(scalar):
            for t in range(NTILE):
                for oc in range(KC):
                    scalar.wait_ge(s_ev, t * EV_T + oc + 1)
                    if oc >= 1:
                        scalar.wait_ge(s_pe, t * PE_T + KC + 1 + oc)
                    scalar.activation(out=sq_f[:], in_=x_f[:, oc, :],
                                      func=AF.Square).then_inc(s_act, 1)
                scalar.wait_ge(s_ev, t * EV_T + KC + 1)
                scalar.activation(out=stat_r[:], in_=stat_r[:], func=AF.Sqrt).then_inc(s_act, 1)

        @block.tensor
        def _(tensor):
            tensor.wait_ge(s_wt, 16)
            for t in range(NTILE):
                tensor.wait_ge(s_ldx, 16 * (t + 1))
                tensor.wait_ge(s_ldo, 16 * (t + 1))
                for oc in range(KC):
                    bank = pmm[oc % 4]
                    if oc >= 4:
                        tensor.wait_ge(s_ev, t * EV_T + oc - 3)
                    elif t >= 1:
                        tensor.wait_ge(s_ev, (t - 1) * EV_T + (KC - 4 + oc) + 1)
                    inst = None
                    for kc in range(KC):
                        inst = tensor.matmul(out=bank[:],
                                             lhsT=wo_bf[:, kc, oc * 128:(oc + 1) * 128],
                                             rhs=o_bf[:, kc, :],
                                             start=(kc == 0), stop=(kc == KC - 1))
                    inst.then_inc(s_pe, 1)
                inst = None
                for oc in range(KC):
                    tensor.wait_ge(s_ev, t * EV_T + oc + 1)
                    inst = tensor.matmul(out=pstat_m[:], lhsT=ones128[:], rhs=x_f[:, oc, :],
                                         start=(oc == 0), stop=(oc == KC - 1))
                inst.then_inc(s_pe, 1)
                for oc in range(KC):
                    tensor.wait_ge(s_act, t * ACT_T + oc + 1)
                    tensor.matmul(out=pstat_q[:], lhsT=ones128[:], rhs=sq_f[:],
                                  start=(oc == 0), stop=(oc == KC - 1)).then_inc(s_pe, 1)
                tensor.wait_ge(s_act, (t + 1) * ACT_T)
                if t >= 1:
                    tensor.wait_ge(s_ev, t * EV_T)
                tensor.matmul(out=pbc[:, 0:TT], lhsT=ones1[:], rhs=stat_m[:],
                              start=True, stop=True)
                tensor.matmul(out=pbc[:, TT:2 * TT], lhsT=ones1[:], rhs=stat_r[:],
                              start=True, stop=True).then_inc(s_pe, 1)

    return nc



def _bitonic_plan(n):
    m = int(math.log2(n))
    assert 2 ** m == n
    e = np.arange(n)
    plan = []
    for k in range(1, m + 1):
        desc = ((e >> k) & 1) == 0
        for j in range(k - 1, -1, -1):
            d = 1 << j
            lower = (e & d) == 0
            keepmax = (lower == desc)
            plan.append((d, keepmax.copy()))
    return plan


def _bitonic_check():
    rng = np.random.default_rng(0)
    v = rng.standard_normal(256)
    for d, keepmax in _bitonic_plan(256):
        prt = v[np.arange(256) ^ d]
        v = np.where(keepmax, np.maximum(v, prt), np.minimum(v, prt))
    assert (np.diff(v) <= 0).all()


def build_l0_attn(TROW=8192, CAP=2176, TT=512):
    HALF = TROW // 2
    KG = TROW // 2
    F = TROW // 128
    NB = 36
    TILES = []
    left = CAP
    while left > 0:
        TILES.append(min(TT, left))
        left -= min(TT, left)
    NSUB = [t // 128 for t in TILES]
    assert all(t % 128 == 0 for t in TILES)
    E = 8
    INNER = 2048
    H1 = 4096

    nc = bass.Bass("TRN2", target_bir_lowering=False, detect_race_conditions=False,
                   dynamic_dma_scratch_size=16384)
    xrow = nc.dram_tensor("xrow", [TROW, D], F32, kind="ExternalInput").ap()
    xrowT = nc.dram_tensor("xrowT", [D, TROW], F32, kind="ExternalInput").ap()
    halfmask_d = nc.dram_tensor("halfmask", [128, 1], F32, kind="ExternalInput").ap()
    hoff_d = nc.dram_tensor("hoff", [128, 1], I32, kind="ExternalInput").ap()
    mw1 = nc.dram_tensor("mw1", [D, 256], F32, kind="ExternalInput").ap()
    mb1 = nc.dram_tensor("mb1", [256], F32, kind="ExternalInput").ap()
    mw2 = nc.dram_tensor("mw2", [256, 1], F32, kind="ExternalInput").ap()
    mb2 = nc.dram_tensor("mb2", [1], F32, kind="ExternalInput").ap()
    ln1w = nc.dram_tensor("ln1w", [D], F32, kind="ExternalInput").ap()
    ln1b = nc.dram_tensor("ln1b", [D], F32, kind="ExternalInput").ap()
    ln2w = nc.dram_tensor("ln2w", [D], F32, kind="ExternalInput").ap()
    ln2b = nc.dram_tensor("ln2b", [D], F32, kind="ExternalInput").ap()
    minw = nc.dram_tensor("minw", [D, 2 * INNER], F32, kind="ExternalInput").ap()
    convw = nc.dram_tensor("convw", [INNER, 4], F32, kind="ExternalInput").ap()
    convb = nc.dram_tensor("convb", [INNER], F32, kind="ExternalInput").ap()
    xprojw = nc.dram_tensor("xprojw", [INNER, 33], F32, kind="ExternalInput").ap()
    moutw = nc.dram_tensor("moutw", [INNER, D], F32, kind="ExternalInput").ap()
    moerw = nc.dram_tensor("moerw", [D, E], F32, kind="ExternalInput").ap()
    moerb = nc.dram_tensor("moerb", [E], F32, kind="ExternalInput").ap()
    moew1 = nc.dram_tensor("moew1", [E, D, H1], F32, kind="ExternalInput").ap()
    moeb1 = nc.dram_tensor("moeb1", [E, H1], F32, kind="ExternalInput").ap()
    moew2 = nc.dram_tensor("moew2", [E, H1, D], F32, kind="ExternalInput").ap()
    moeb2 = nc.dram_tensor("moeb2", [E, D], F32, kind="ExternalInput").ap()
    y_tm = nc.dram_tensor("y_tm", [TROW, D], F32, kind="ExternalOutput").ap()

    import os as _os2
    _dbg = _os2.environ.get("L0_DBG", "") == "1"
    _dbg2 = _os2.environ.get("L0_DBG2", "") == "1"
    _K = "ExternalOutput" if _dbg else "Internal"
    if _dbg2:
        d_gtok = nc.dram_tensor("d_gtok", [128, D], F32, kind="ExternalOutput").ap()
        d_stats = nc.dram_tensor("d_stats", [128, 2], F32, kind="ExternalOutput").ap()
        d_n1 = nc.dram_tensor("d_n1", [128, KC, TT], BF16, kind="ExternalOutput").ap()
        d_tokf = nc.dram_tensor("d_tokf", [128, KC, TT], BF16, kind="ExternalOutput").ap()
        d_moe = nc.dram_tensor("d_moe", [128, KC, CAP], BF16, kind="ExternalOutput").ap()
        d_gate = nc.dram_tensor("d_gate", [8, CAP], BF16, kind="ExternalOutput").ap()
        d_n1tm = nc.dram_tensor("d_n1tm", [128, D], BF16, kind="ExternalOutput").ap()
        d_tokb = nc.dram_tensor("d_tokb", [128, D], BF16, kind="ExternalOutput").ap()
        d_bank = nc.dram_tensor("d_bank", [128, 128], BF16, kind="ExternalOutput").ap()
        d_idbs = nc.dram_tensor("d_idbs", [18, 128, 128], BF16, kind="ExternalOutput").ap()
        d_src1 = nc.dram_tensor("d_src1", [128, D], BF16, kind="ExternalOutput").ap()
    sc_scores = nc.dram_tensor("sc_scores", [TROW], F32, kind=_K).ap()
    sc_sl = nc.dram_tensor("sc_sl", [KG], F32, kind=_K).ap()
    sc_wv = nc.dram_tensor("sc_wv", [KG + CAP], F32, kind=_K).ap()
    sc_wrow = nc.dram_tensor("sc_wrow", [CAP], F32, kind=_K).ap()
    sc_S = nc.dram_tensor("sc_S", [TROW + 1, D], F32, kind=_K).ap()
    sc_n2 = nc.dram_tensor("sc_n2", [128, KC, CAP], BF16, kind=_K).ap()


    ones1_d = nc.inline_tensor(np.ones((1, 128), np.float32), "ones1c").ap()
    idb_d = nc.inline_tensor(np.eye(128, dtype=ml_dtypes.bfloat16), "idbc").ap()
    idf_d = nc.inline_tensor(np.eye(128, dtype=np.float32), "idfc").ap()
    iota128_d = nc.inline_tensor(np.arange(128, dtype=np.int32).reshape(128, 1), "iota128").ap()
    plan = _bitonic_plan(TROW)
    FS = F
    xdps = sorted({dd // FS for (dd, _) in plan if dd >= FS})
    perm_np = np.zeros((128, len(xdps) * 128), np.float32)
    for j, dp in enumerate(xdps):
        for q in range(128):
            perm_np[q, j * 128 + (q ^ dp)] = 1.0
    km_np = np.stack([km.reshape(128, FS) for (_, km) in plan], axis=1).astype(np.uint8)
    iotaf_np = np.arange(TROW, dtype=np.float32).reshape(128, F)
    fh_np = np.zeros((128, 1), np.float32); fh_np[0:64] = 1.0
    km_d = nc.inline_tensor(km_np, "sortmask").ap()
    perm_d = nc.inline_tensor(perm_np, "sortperm").ap()
    iotaf_d = nc.inline_tensor(iotaf_np, "iotaf").ap()
    fh_d = nc.inline_tensor(fh_np, "fhmask").ap()
    oneh_np = np.zeros((E, E * 128), ml_dtypes.bfloat16)
    for e_ in range(E):
        oneh_np[e_, e_ * 128:(e_ + 1) * 128] = 1.0
    oneh_d = nc.inline_tensor(oneh_np, "onehot").ap()

    with ExitStack() as ctx:
        sb = lambda name, shape, dt=F32: ctx.enter_context(nc.sbuf_tensor(name, shape, dt))
        ps = lambda name, shape, dt=F32: ctx.enter_context(nc.psum_tensor(name, shape, dt))
        sem = lambda name: ctx.enter_context(nc.semaphore(name))

        iotaf_s = sb("iotaf_s", [128, F])
        fh_s = sb("fh_s", [128, 1])
        ones1_s = sb("ones1_s", [1, 128])
        ones128 = sb("ones128", [128, 1])
        ones128b = sb("ones128b", [128, 1], BF16)
        idb_s = sb("idb_s", [128, 128], BF16)
        idf_s = sb("idf_s", [128, 128])
        lgt_fm = sb("lgt_fm", [8, TT])
        iota128_s = sb("iota128_s", [128, 1], I32)
        km_s = sb("km_s", [128, len(plan), FS], mybir.dt.uint8)
        perm_s = sb("perm_s", [128, len(xdps) * 128])
        oneh_s = sb("oneh_s", [8, E * 128], BF16)
        halfm_s = sb("halfm_s", [128, 1])
        hoff_s = sb("hoff_s", [128, 1], I32)
        sco = sb("sco", [128, F])
        mwb_f = sb("mwb_f", [128, 2, 1])
        mba_s = sb("mba_s", [128, 2])
        mbb_s = sb("mbb_s", [1, 1])
        lo_s = sb("lo_s", [128, 1])
        hi_s = sb("hi_s", [128, 1])
        mid_s = sb("mid_s", [128, 1])
        geb_s = sb("geb_s", [128, 1])
        tmp1_s = sb("tmp1_s", [128, 1])
        cnt_s = sb("cnt_s", [1, 1])
        maskf = sb("maskf", [128, F])
        mmy = sb("mmy", [128, F])
        srt = sb("srt", [128, FS])
        prt = sb("prt", [128, FS])
        smn = sb("smn", [128, FS])
        smx = sb("smx", [128, FS])
        w64 = sb("w64", [64, F])
        sl64 = sb("sl64", [64, F])
        bigt = sb("bigt", [128, ceil_div(CAP, 128)])
        wmax = sb("wmax", [128, 1])
        wsum = sb("wsum", [1, 1])
        wrec = sb("wrec", [128, 1])
        j0c = sb("j0c", [128, 1])
        cntc = sb("cntc", [1, 1])
        widxf = sb("widxf", [128, 1])
        widxi = sb("widxi", [128, 1], I32)
        swsl = sb("swsl", [128, ceil_div(CAP, 128)])
        sww = sb("sww", [128, ceil_div(CAP, 128)])
        swsi = sb("swsi", [128, ceil_div(CAP, 128)], I32)
        swsg = sb("swsg", [128, ceil_div(CAP, 128)], I32)
        widx2 = sb("widx2", [128, 1], I32)
        w_row = sb("w_row", [1, 128])
        wrow_t = sb("wrow_t", [1, TT])
        wbig = sb("wbig", [128, KC, 2 * INNER], BF16)
        mos = sb("mos", [128, 2, 256], BF16)
        w2s = sb("w2s", [128, 2, 2, D], BF16)
        scrA = sb("scrA", [128, 16, TT], BF16)
        scrB = sb("scrB", [128, 16, TT], BF16)
        n1fm = sb("n1fm", [128, KC, TT], BF16)
        tokfm = sb("tokfm", [128, KC, TT], BF16)
        n2str = sb("n2str", [128, KC, TT], BF16)
        moeacc = sb("moeacc", [128, KC, CAP], BF16)
        gtok = sb("gtok", [128, D])
        gsq = sb("gsq", [128, D])
        n1tm_b = sb("n1tm_b", [128, D], BF16)
        tok_b = sb("tok_b", [128, D], BF16)
        mu_s = sb("mu_s", [128, 1])
        rs_s = sb("rs_s", [128, 1])
        mcol = sb("mcol", [128, 1])
        ln1w_s = sb("ln1w_s", [128, KC])
        ln1b_s = sb("ln1b_s", [128, KC])
        ln2w_s = sb("ln2w_s", [128, KC])
        ln2b_s = sb("ln2b_s", [128, KC])
        convw_s = sb("convw_s", [128, 16, 4])
        convb_s = sb("convb_s", [128, 16])
        wBC_b = sb("wBC_b", [128, 16, 2], BF16)
        wBC_f = sb("wBC_f", [128, 16, 2])
        xpj_s = sb("xpj_s", [128, 16, 33])
        moer_b = sb("moer_b", [128, KC, E], BF16)
        moerb_s = sb("moerb_s", [8, 1])
        b1e_s = sb("b1e_s", [128, 32])
        b2e_s = sb("b2e_s", [128, KC])
        sB_s = sb("sB_s", [1, TT])
        ysc_s = sb("ysc_s", [1, TT])
        gate_fm = sb("gate_fm", [8, CAP], BF16)
        lgt = sb("lgt", [128, 4, 8])
        lex = sb("lex", [128, 4, 8])
        lsum = sb("lsum", [128, 4])
        lm1 = sb("lm1", [128, 4])
        lm2 = sb("lm2", [128, 4])
        lden = sb("lden", [128, 4])
        ltmp = sb("ltmp", [128, 4, 8])

        gatebc = n1tm_b.ap()[:, 0:TT]
        _kmf = km_s.ap().rearrange("p a b -> p (a b)").bitcast(F32)
        stat_m = _kmf[0:1, 0:TT]
        stat_r = _kmf[0:1, TT:2 * TT]
        xs_f = scrA.ap().rearrange("p a b -> p (a b)").bitcast(F32).rearrange(
            "p (a b) -> p a b", a=KC)[:, :, 0:TT]
        hr_f = scrB.ap().rearrange("p a b -> p (a b)").bitcast(F32).rearrange(
            "p (a b) -> p a b", a=2)[:, :, 0:TT]
        mwa_f = w2s.ap().rearrange("p a b n -> p (a b n)").bitcast(F32).rearrange(
            "p (a b) -> p a b", a=KC)[:, :, 0:256]
        gx = n1fm.ap().rearrange("p a b -> p (a b)").bitcast(F32)[:, 0:D]
        gS = tokfm.ap().rearrange("p a b -> p (a b)").bitcast(F32)[:, 0:D]

        pR = [ps(f"pR{i}", [128, TT]) for i in range(4)]
        pM = [ps(f"pM{i}", [128, TT]) for i in range(2)]
        pX = ps("pX", [128, TT])
        pW = ps("pW", [128, TT])

        sems = {n: sem("s0_" + n) for n in ("sync", "gpsimd", "tensor", "vector", "scalar")}
        sd = Sched()

        def step(engine, deps, emit, inc=1):
            return sd.step(engine, deps, emit, inc)

        def vseq(deps, emits):
            p = None
            for em in emits:
                p = sd.step("vector", (deps if p is None else [p]), em)
            return p

        def idb_dump(i, deps):
            if not _dbg2:
                return None
            return step("sync", deps, (lambda i=i: (lambda e: e.dma_start(
                out=d_idbs[i], in_=idb_s[:])))(), inc=16)

        cl = []
        for dst, srcd in ((iotaf_s, iotaf_d), (fh_s, fh_d),
                          (ones1_s, ones1_d), (idb_s, idb_d), (iota128_s, iota128_d),
                          (km_s, km_d), (oneh_s, oneh_d), (halfm_s, halfmask_d),
                          (hoff_s, hoff_d), (idf_s, idf_d), (perm_s, perm_d)):
            cl.append(step("sync", [], (lambda dst=dst, srcd=srcd: (lambda e: e.dma_start(out=dst[:], in_=srcd)))(), inc=16))
        CONSTS = cl[-1]
        ld_mwa = step("sync", [], lambda e: e.dma_start(out=mwa_f[:], in_=mw1.rearrange("(kc p) n -> p kc n", p=128)), inc=16)
        ld_mwb = step("sync", [], lambda e: e.dma_start(out=mwb_f[:], in_=mw2.rearrange("(kc p) n -> p kc n", p=128), allow_slow_non_contiguous=True), inc=16)
        ld_mba = step("sync", [], lambda e: e.dma_start(out=mba_s[:, 0:1], in_=mb1[0:128].unsqueeze(1)), inc=16)
        ld_mba2 = step("sync", [ld_mba], lambda e: e.dma_start(out=mba_s[:, 1:2], in_=mb1[128:256].unsqueeze(1)), inc=16)
        ld_mbb = step("sync", [], lambda e: e.dma_start(out=mbb_s[:], in_=mb2.unsqueeze(0)), inc=16)
        ld_ln1 = []
        for c in range(KC):
            ld_ln1.append(step("sync", [], (lambda c=c: (lambda e: e.dma_start(
                out=ln1w_s[:, c:c + 1], in_=ln1w[c * 128:(c + 1) * 128].unsqueeze(1))))(), inc=16))
            ld_ln1.append(step("sync", [], (lambda c=c: (lambda e: e.dma_start(
                out=ln1b_s[:, c:c + 1], in_=ln1b[c * 128:(c + 1) * 128].unsqueeze(1))))(), inc=16))
        ld_convw = step("sync", [], lambda e: e.dma_start(out=convw_s[:], in_=convw.rearrange("(c p) f -> p c f", p=128)), inc=16)
        ld_convb = []
        for c in range(16):
            ld_convb.append(step("sync", [], (lambda c=c: (lambda e: e.dma_start(
                out=convb_s[:, c:c + 1], in_=convb[c * 128:(c + 1) * 128].unsqueeze(1))))(), inc=16))
        ld_xpj = step("sync", [], lambda e: e.dma_start(out=xpj_s[:], in_=xprojw.rearrange("(c p) f -> p c f", p=128)), inc=16)
        ld_moer = step("gpsimd", [], lambda e: e.dma_start(out=moer_b[:], in_=moerw.rearrange("(kc p) n -> p kc n", p=128)), inc=16)
        ld_moerb = step("sync", [], lambda e: e.dma_start(out=moerb_s[:], in_=moerb.unsqueeze(1)), inc=16)
        ld_ln2 = []
        for c in range(KC):
            ld_ln2.append(step("sync", [], (lambda c=c: (lambda e: e.dma_start(
                out=ln2w_s[:, c:c + 1], in_=ln2w[c * 128:(c + 1) * 128].unsqueeze(1))))(), inc=16))
            ld_ln2.append(step("sync", [], (lambda c=c: (lambda e: e.dma_start(
                out=ln2b_s[:, c:c + 1], in_=ln2b[c * 128:(c + 1) * 128].unsqueeze(1))))(), inc=16))
        ld_minw = step("gpsimd", [], lambda e: e.dma_start(out=wbig[:], in_=minw.rearrange("(kc p) n -> p kc n", p=128)), inc=16)
        idb_dump(0, [CONSTS])

        init_v = step("vector", [CONSTS], lambda e: (
            e.memset(ones128[:], 1.0), e.memset(ones128b[:], 1.0),
            e.memset(gsq[:], 0.0),
            e.memset(lo_s[:], -16.0), e.memset(hi_s[:], 16.0))[-1])

        v_ln1bc2 = init_v

        v_wbc = step("vector", [ld_xpj], lambda e: (
            e.reduce_sum(out=wBC_f[:, :, 0:1], in_=xpj_s[:, :, 16:32], axis=AX.X),
            e.tensor_copy(out=wBC_f[:, :, 1:2], in_=xpj_s[:, :, 32:33]),
            e.tensor_copy(out=wBC_b[:], in_=wBC_f[:]))[-1])

        NTS = TROW // TT
        sc_done = []
        for t in range(NTS):
            t0 = t * TT
            deps = [ld_mwa, ld_mba2, ld_mbb, CONSTS, v_ln1bc2]
            if t >= 1:
                deps.append(sc_done[-1])
            ld = step("sync", deps, (lambda t0=t0: (lambda e: e.dma_start(
                out=xs_f[:], in_=xrowT[:, t0:t0 + TT].rearrange("(kc p) t -> p kc t", p=128))))(), inc=16)
            mm1 = step("tensor", [ld], (lambda: (lambda e: [
                [e.matmul(out=pR[og][:], lhsT=mwa_f[:, kc, og * 128:(og + 1) * 128], rhs=xs_f[:, kc, :],
                          start=(kc == 0), stop=(kc == KC - 1)) for kc in range(KC)]
                for og in range(2)][-1][-1]))())
            ev1 = step("scalar", [mm1], lambda e: (
                e.activation(out=hr_f[:, 0, :], in_=pR[0][:], func=AF.Relu, bias=mba_s[:, 0:1]),
                e.activation(out=hr_f[:, 1, :], in_=pR[1][:], func=AF.Relu, bias=mba_s[:, 1:2]))[-1])
            mm2 = step("tensor", [ev1], lambda e: (
                e.matmul(out=pX[0:1, 0:TT], lhsT=mwb_f[:, 0, :], rhs=hr_f[:, 0, :], start=True, stop=False),
                e.matmul(out=pX[0:1, 0:TT], lhsT=mwb_f[:, 1, :], rhs=hr_f[:, 1, :], start=False, stop=True))[-1])
            ev2 = step("vector", [mm2], lambda e: e.tensor_scalar(
                out=hr_f[0:1, 0, :], in0=pX[0:1, 0:TT], scalar1=mbb_s[0:1, 0:1], scalar2=None, op0=ALU.add))
            st2 = step("sync", [ev2], (lambda t0=t0: (lambda e: e.dma_start(
                out=sc_scores[t0:t0 + TT].unsqueeze(0), in_=hr_f[0:1, 0, :])))(), inc=16)
            sc_done.append(st2)

        ld_sco = step("sync", [sc_done[-1]], lambda e: e.dma_start(
            out=sco[:], in_=sc_scores.rearrange("(p f) -> p f", p=128)), inc=16)
        idb_dump(1, [ld_sco])

        bis_prev = init_v
        for it in range(NB):
            v_mid = vseq([ld_sco, bis_prev], [
                lambda e: e.tensor_add(out=mid_s[:], in0=lo_s[:], in1=hi_s[:]),
                lambda e: e.tensor_scalar(out=mid_s[:], in0=mid_s[:], scalar1=0.5, scalar2=None, op0=ALU.mult),
                lambda e: e.tensor_scalar(out=maskf[:], in0=sco[:], scalar1=mid_s[:, 0:1], scalar2=None, op0=ALU.is_gt)])
            t_cnt = step("tensor", [v_mid], lambda e: e.matmul(
                out=pX[0:1, 0:F], lhsT=ones128[:], rhs=maskf[:], start=True, stop=True))
            v_cnt = vseq([t_cnt], [
                lambda e: e.reduce_sum(out=cnt_s[:], in_=pX[0:1, 0:F], axis=AX.X),
                lambda e: e.tensor_scalar(out=cnt_s[:], in0=cnt_s[:], scalar1=float(KG), scalar2=None, op0=ALU.is_ge)])
            t_bc = step("tensor", [v_cnt], lambda e: e.matmul(
                out=pX[0:1, 256:384] if False else pW[:, 0:1], lhsT=ones1_s[:], rhs=cnt_s[:], start=True, stop=True))
            bis_prev = vseq([t_bc], [
                lambda e: e.tensor_copy(out=geb_s[:], in_=pW[:, 0:1]),
                lambda e: e.tensor_sub(out=tmp1_s[:], in0=mid_s[:], in1=lo_s[:]),
                lambda e: e.tensor_mul(out=tmp1_s[:], in0=tmp1_s[:], in1=geb_s[:]),
                lambda e: e.tensor_add(out=lo_s[:], in0=lo_s[:], in1=tmp1_s[:]),
                lambda e: e.tensor_sub(out=tmp1_s[:], in0=hi_s[:], in1=mid_s[:]),
                lambda e: e.tensor_scalar(out=geb_s[:], in0=geb_s[:], scalar1=-1.0, scalar2=1.0,
                                          op0=ALU.mult, op1=ALU.add),
                lambda e: e.tensor_mul(out=tmp1_s[:], in0=tmp1_s[:], in1=geb_s[:]),
                lambda e: e.tensor_sub(out=hi_s[:], in0=hi_s[:], in1=tmp1_s[:])])

        v_mask = vseq([bis_prev], [
            lambda e: e.tensor_scalar(out=maskf[:], in0=sco[:], scalar1=lo_s[:, 0:1], scalar2=None, op0=ALU.is_gt),
            lambda e: e.tensor_scalar(out=mmy[:], in0=maskf[:], scalar1=halfm_s[:, 0:1], scalar2=None, op0=ALU.mult)])

        idb_dump(2, [v_mask])
        v_j0a = step("vector", [v_mask], lambda e: e.tensor_scalar(
            out=srt[:], in0=maskf[:], scalar1=fh_s[:, 0:1], scalar2=None, op0=ALU.mult))
        t_j0 = step("tensor", [v_j0a], lambda e: e.matmul(
            out=pW[0:1, 0:F], lhsT=ones128[:], rhs=srt[:], start=True, stop=True))
        v_j0b = step("vector", [t_j0], lambda e: e.reduce_sum(
            out=cntc[:], in_=pW[0:1, 0:F], axis=AX.X))
        t_j0b = step("tensor", [v_j0b], lambda e: e.matmul(
            out=pW[:, 0:1], lhsT=ones1_s[:], rhs=cntc[:], start=True, stop=True))
        v_j0c = vseq([t_j0b], [
            lambda e: e.tensor_copy(out=j0c[:], in_=pW[:, 0:1]),
            lambda e: e.tensor_copy(out=widxf[:], in_=hoff_s[:]),
            lambda e: e.tensor_scalar(out=widxf[:], in0=widxf[:], scalar1=1.0 / HALF, scalar2=None, op0=ALU.mult),
            lambda e: e.tensor_mul(out=j0c[:], in0=j0c[:], in1=widxf[:])])

        def run_sort(first_deps, init_emit):
            p = step("vector", list(first_deps), init_emit)
            for si, (dd, _) in enumerate(plan):
                if dd < FS:
                    v_prt = step("vector", [p], (lambda dd=dd: (lambda e: (
                        e.tensor_copy(out=prt.rearrange("p (b two d) -> p b two d", two=2, d=dd)[:, :, 0, :],
                                      in_=srt.rearrange("p (b two d) -> p b two d", two=2, d=dd)[:, :, 1, :]),
                        e.tensor_copy(out=prt.rearrange("p (b two d) -> p b two d", two=2, d=dd)[:, :, 1, :],
                                      in_=srt.rearrange("p (b two d) -> p b two d", two=2, d=dd)[:, :, 0, :]))[-1]))())
                else:
                    dp = dd // FS
                    j = xdps.index(dp)
                    t_pm = step("tensor", [p, CONSTS], (lambda j=j: (lambda e: e.matmul(
                        out=pW[:, 0:FS], lhsT=perm_s[:, j * 128:(j + 1) * 128], rhs=srt[:],
                        start=True, stop=True)))())
                    v_prt = step("vector", [t_pm], lambda e: e.tensor_copy(out=prt[:], in_=pW[:, 0:FS]))
                p = step("vector", [v_prt], (lambda si=si: (lambda e: (
                    e.tensor_max(out=smx[:], in0=srt[:], in1=prt[:]),
                    e.tensor_tensor(out=smn[:], in0=srt[:], in1=prt[:], op=ALU.min),
                    e.select(out=srt[:], mask=km_s[:, si, :], on_true=smx[:], on_false=smn[:]))[-1]))())
            return p

        s1_pre = vseq([v_mask, t_j0, v_j0c], [
            lambda e: e.tensor_scalar(out=prt[:], in0=maskf[:], scalar1=1000000.0, scalar2=-1000000.0,
                                      op0=ALU.mult, op1=ALU.add)])
        s1_end = run_sort([s1_pre], lambda e: e.tensor_add(out=srt[:], in0=sco[:], in1=prt[:]))
        idb_dump(3, [s1_end])
        t_wmx = step("tensor", [s1_end], lambda e: e.matmul(
            out=pW[:, 0:1], lhsT=ones1_s[:], rhs=srt[0:1, 0:1], start=True, stop=True))
        v_wexp0 = vseq([t_wmx], [
            lambda e: e.tensor_copy(out=wmax[:], in_=pW[:, 0:1]),
            lambda e: e.tensor_scalar(out=wmax[:], in0=wmax[:], scalar1=-1.0, scalar2=None, op0=ALU.mult)])
        s_wexp = step("scalar", [v_wexp0], lambda e: e.activation(
            out=w64[:], in_=srt[0:64, :], func=AF.Exp, bias=wmax[0:64, 0:1], scale=1.0))
        t_wsm = step("tensor", [s_wexp], lambda e: e.matmul(
            out=pW[0:1, 0:F], lhsT=ones128[0:64, :], rhs=w64[:], start=True, stop=True,
            tile_position=(0, 0)))
        v_wsum = vseq([t_wsm], [
            lambda e: e.reduce_sum(out=wsum[:], in_=pW[0:1, 0:F], axis=AX.X),
            lambda e: e.reciprocal(out=wsum[:], in_=wsum[:])])
        t_wrc = step("tensor", [v_wsum], lambda e: e.matmul(
            out=pW[:, 0:1], lhsT=ones1_s[:], rhs=wsum[:], start=True, stop=True))
        v_wfin = vseq([t_wrc], [
            lambda e: e.tensor_copy(out=wrec[:], in_=pW[:, 0:1]),
            lambda e: e.tensor_scalar(out=w64[:], in0=w64[:], scalar1=wrec[0:64, 0:1], scalar2=None, op0=ALU.mult)])

        s2_pre = vseq([v_wfin], [
            lambda e: e.tensor_scalar(out=prt[:], in0=mmy[:], scalar1=1000000.0, scalar2=-1000000.0,
                                      op0=ALU.mult, op1=ALU.add),
            lambda e: e.tensor_scalar(out=smx[:], in0=iotaf_s[:], scalar1=-1.0, scalar2=None, op0=ALU.mult),
            lambda e: e.tensor_mul(out=smx[:], in0=smx[:], in1=mmy[:])])
        s2_end = run_sort([s2_pre], lambda e: e.tensor_add(out=srt[:], in0=smx[:], in1=prt[:]))
        idb_dump(4, [s2_end])

        v_pair = vseq([s2_end], [
            lambda e: e.tensor_scalar(out=sl64[:], in0=srt[0:64, :], scalar1=-1.0, scalar2=None, op0=ALU.mult),
            lambda e: e.memset(bigt[:], 0.0)])
        st_sl = step("sync", [v_pair], lambda e: e.dma_start(
            out=sc_sl.rearrange("(p f) -> p f", p=64), in_=sl64[:]), inc=16)
        st_wv = step("sync", [v_pair], lambda e: e.dma_start(
            out=sc_wv[0:KG].rearrange("(p f) -> p f", p=64), in_=w64[:]), inc=16)
        st_wvp = step("sync", [v_pair], lambda e: e.dma_start(
            out=sc_wv[KG:KG + CAP].rearrange("(p f) -> p f", p=128), in_=bigt[:]), inc=16)

        NCC = ceil_div(CAP, 128)
        v_widx = vseq([v_j0c, CONSTS], [
            lambda e: e.tensor_copy(out=widxf[:], in_=iota128_s[:]),
            lambda e: e.tensor_scalar(out=widxf[:], in0=widxf[:], scalar1=float(NCC),
                                      scalar2=None, op0=ALU.mult),
            lambda e: e.tensor_copy(out=widxi[:], in_=widxf[:]),
            lambda e: e.tensor_add(out=widxf[:], in0=widxf[:], in1=j0c[:]),
            lambda e: e.tensor_copy(out=widx2[:], in_=widxf[:])])
        fl_w = step("gpsimd", [ld_minw, ld_moer],
                    lambda e: e.dma_start(out=iota128_s[:], in_=iota128_d), inc=16)
        g_win1 = step("gpsimd", [v_widx, st_sl, fl_w], lambda e: e.indirect_dma_start(
            out=swsl[:], out_offset=None, in_=sc_sl.unsqueeze(1),
            in_offset=IndirectOffsetOnAxis(ap=widxi[:, 0:1], axis=0),
            bounds_check=KG - 1, oob_is_err=False), inc=16)
        g_win2 = step("gpsimd", [v_widx, st_wv, st_wvp], lambda e: e.indirect_dma_start(
            out=sww[:], out_offset=None, in_=sc_wv.unsqueeze(1),
            in_offset=IndirectOffsetOnAxis(ap=widx2[:, 0:1], axis=0),
            bounds_check=KG + CAP - 1, oob_is_err=False), inc=16)
        v_swsi = vseq([g_win1], [
            lambda e: e.tensor_scalar(out=swsl[:], in0=swsl[:], scalar1=float(TROW),
                                      scalar2=None, op0=ALU.min),
            lambda e: e.tensor_copy(out=swsi[:], in_=swsl[:]),
            lambda e: e.tensor_scalar(out=swsl[:], in0=swsl[:], scalar1=float(TROW - 1),
                                      scalar2=None, op0=ALU.min),
            lambda e: e.tensor_copy(out=swsg[:], in_=swsl[:])])
        wr_prev = None
        for g in range(NCC):
            t_wt = step("tensor", [g_win2, CONSTS] + ([wr_prev] if wr_prev else []),
                        (lambda g=g: (lambda e: e.transpose(
                            out=pX[0:1, 0:128], in_=sww[:, g:g + 1], identity=idf_s[:])))())
            v_wt = step("vector", [t_wt], lambda e: e.tensor_copy(
                out=w_row[0:1, :], in_=pX[0:1, 0:128]))
            wr_prev = step("sync", [v_wt], (lambda g=g: (lambda e: e.dma_start(
                out=sc_wrow[g * 128:(g + 1) * 128].unsqueeze(0), in_=w_row[:])))(), inc=16)
        WROW_DONE = wr_prev
        SC_DONE = v_swsi
        idb_dump(5, [SC_DONE, WROW_DONE])

        zS = []
        for r in range(TROW // 128):
            zS.append(step("sync", [init_v],
                           (lambda r=r: (lambda e: e.dma_start(
                               out=sc_S[r * 128:(r + 1) * 128, :], in_=gsq[:])))(), inc=16))

        gATH = []
        prevA = {}
        tilestart = [sum(TILES[:i]) for i in range(len(TILES))]
        last_phaseA_tensor = None
        for ti, TW in enumerate(TILES):
            c0 = tilestart[ti]
            nsub = TW // 128
            prevA["tr_i"] = 0
            prevA["tr_bank0"] = prevA.get("tr_ev")
            prevA["tr_bank1"] = prevA.get("tr_ev")
            if ti == 1:
                idb_dump(9, [prevA["tr_ev"]] if prevA.get("tr_ev") is not None else [CONSTS])
            for su in range(nsub):
                gcol = (c0 + su * 128) // 128
                gdeps = [SC_DONE] + ld_ln1[-1:]
                if "g_used" in prevA:
                    gdeps += prevA["g_used"]
                if su == 0 and ti > 0:
                    fl = step("gpsimd", [prevA[f"moev_last_{ti - 1}"]],
                              lambda e: e.dma_start(out=iota128_s[:], in_=iota128_d), inc=16)
                    gdeps.append(fl)
                    if ti == 1 and prevA.get("ep8") is not None:
                        gdeps.append(prevA["ep8"])
                g1 = step("gpsimd", gdeps, (lambda gcol=gcol: (lambda e: e.indirect_dma_start(
                    out=gtok[:], out_offset=None, in_=xrow,
                    in_offset=IndirectOffsetOnAxis(ap=swsg[:, gcol:gcol + 1], axis=0),
                    bounds_check=TROW - 1, oob_is_err=False)))(), inc=16)
                vst = vseq([g1], [
                    lambda e: e.reduce_sum(out=mu_s[:], in_=gtok[:], axis=AX.X),
                    lambda e: e.tensor_scalar(out=mu_s[:], in0=mu_s[:], scalar1=1.0 / D, scalar2=None, op0=ALU.mult)])
                sstdeps = [vst]
                if "gsq_zeroed" not in prevA:
                    sstdeps.append(zS[-1])
                    prevA["gsq_zeroed"] = True
                sst = step("scalar", sstdeps, lambda e: e.activation(out=gsq[:], in_=gtok[:], func=AF.Square))
                vst2 = vseq([sst], [
                    lambda e: e.reduce_sum(out=rs_s[:], in_=gsq[:], axis=AX.X),
                    lambda e: e.tensor_scalar(out=rs_s[:], in0=rs_s[:], scalar1=1.0 / D, scalar2=None, op0=ALU.mult),
                    lambda e: e.tensor_mul(out=tmp1_s[:], in0=mu_s[:], in1=mu_s[:]),
                    lambda e: e.tensor_sub(out=rs_s[:], in0=rs_s[:], in1=tmp1_s[:]),
                    lambda e: e.tensor_scalar(out=rs_s[:], in0=rs_s[:], scalar1=float(EPS), scalar2=None, op0=ALU.add),
                    lambda e: e.reciprocal(out=rs_s[:], in_=rs_s[:])])
                sst2 = step("scalar", [vst2], lambda e: e.activation(out=rs_s[:], in_=rs_s[:], func=AF.Sqrt))
                vn1deps = [sst2]
                if "tr_last_t" in prevA:
                    vn1deps.append(prevA["tr_last_t"])
                if "vn1_extra" in prevA:
                    vn1deps += prevA.pop("vn1_extra")
                vn1a = step("vector", vn1deps, lambda e: e.tensor_scalar(
                    out=n1tm_b[:], in0=gtok[:], scalar1=mu_s[:, 0:1], scalar2=rs_s[:, 0:1],
                    op0=ALU.subtract, op1=ALU.mult))
                vn1 = step("vector", [vn1a], lambda e: e.tensor_copy(out=tok_b[:], in_=gtok[:]))
                prevA["g_used"] = [vn1]
                if _dbg2 and ti == 1 and su == 0:
                    dg = step("sync", [g1], lambda e: e.dma_start(out=d_gtok, in_=gtok[:]), inc=16)
                    dsc = step("vector", [sst2, vn1], lambda e: (
                        e.tensor_copy(out=mba_s[:, 0:1], in_=mu_s[:]),
                        e.tensor_copy(out=mba_s[:, 1:2], in_=rs_s[:]))[-1])
                    step("sync", [dsc], lambda e: e.dma_start(out=d_stats, in_=mba_s[:]), inc=16)
                    dnm = step("sync", [vn1], lambda e: e.dma_start(out=d_n1tm, in_=n1tm_b[:]), inc=16)
                    dtb = step("sync", [dnm], lambda e: e.dma_start(out=d_tokb, in_=tok_b[:]), inc=16)
                    prevA.setdefault("vn1_extra", []).append(dtb)
                    ds1 = step("sync", [vn1, prevA["gate_0"]],
                               lambda e: e.dma_start(out=d_src1, in_=n1tm_b[:]), inc=16)
                    prevA["vn1_extra"].append(ds1)
                    prevA["g_used"] = [vn1, dg]
                pend = None
                for kc in range(KC):
                    for which in range(2):
                        src_t = n1tm_b if which == 0 else tok_b
                        b = prevA["tr_i"] % 2
                        bank = pX if b == 0 else pW
                        tdeps = [vn1a if which == 0 else vn1]
                        if prevA.get(f"tr_bank{b}") is not None:
                            tdeps.append(prevA[f"tr_bank{b}"])
                        if prevA["tr_i"] == 0 and su == 0:
                            tdeps += [CONSTS, WROW_DONE]
                            if ti > 0:
                                tdeps.append(prevA[f"n2st_{ti - 1}"])
                        tstep = step("tensor", tdeps, (lambda src_t=src_t, kc=kc, bank=bank: (lambda e: e.transpose(
                            out=bank.bitcast(BF16)[:, 0:128], in_=src_t[:, kc * 128:(kc + 1) * 128],
                            identity=idb_s[:])))())
                        if pend is not None:
                            pf, pbnk = pend
                            ev = step("vector", [tstep], pf)
                            prevA[f"tr_bank{pbnk}"] = ev
                            prevA["tr_ev"] = ev
                            if _dbg2 and ti == 1 and su == 0 and prevA["tr_i"] == 1:
                                rr = step("vector", [ev], lambda e: e.tensor_copy(
                                    out=mos[:, 0, 0:128], in_=pX.bitcast(BF16)[:, 0:128]))
                                prevA["tr_bank0"] = rr
                                prevA["mos_dbg"] = step("sync", [rr], lambda e: e.dma_start(
                                    out=d_bank, in_=mos[:, 0, 0:128]), inc=16)
                        if which == 0:
                            evf = (lambda kc=kc, su=su, bank=bank: (lambda e: e.tensor_scalar(
                                out=n1fm[:, kc, su * 128:(su + 1) * 128], in0=bank.bitcast(BF16)[:, 0:128],
                                scalar1=ln1w_s[:, kc:kc + 1], scalar2=ln1b_s[:, kc:kc + 1],
                                op0=ALU.mult, op1=ALU.add)))()
                        else:
                            evf = (lambda kc=kc, su=su, bank=bank: (lambda e: e.tensor_copy(
                                out=tokfm[:, kc, su * 128:(su + 1) * 128], in_=bank.bitcast(BF16)[:, 0:128])))()
                        pend = (evf, b)
                        prevA["tr_last_t"] = tstep
                        prevA["tr_i"] += 1
                dmy = step("tensor", [prevA[f"tr_bank{pend[1]}"]] if prevA.get(f"tr_bank{pend[1]}") else [prevA["tr_last_t"]],
                           lambda e: e.transpose(out=pM[0].bitcast(BF16)[:, 0:128],
                                                 in_=idb_s[:, 0:128], identity=idb_s[:]))
                pf, pbnk = pend
                ev = step("vector", [dmy], pf)
                prevA[f"tr_bank{pbnk}"] = ev
                prevA["tr_ev"] = ev
            gATH.append(prevA["tr_ev"])
            if ti == 0:
                idb_dump(6, [prevA["tr_ev"]])
            tr_all = prevA["tr_ev"]
            for og in range(32):
                bank = pR[og % 4]
                mdeps = [tr_all, ld_minw, ld_convw] + ld_convb[-1:]
                if f"minev{og - 4}_{ti}" in prevA:
                    mdeps.append(prevA[f"minev{og - 4}_{ti}"])
                elif ti > 0:
                    mdeps.append(prevA[f"minev{og + 28}_{ti - 1}"])
                mm = step("tensor", mdeps, (lambda og=og, TW=TW: (lambda e: [
                    e.matmul(out=pR[og % 4][:, 0:TW], lhsT=wbig[:, kc, og * 128:(og + 1) * 128],
                             rhs=n1fm[:, kc, 0:TW], start=(kc == 0), stop=(kc == KC - 1))
                    for kc in range(KC)][-1]))())
                if og < 16:
                    ev = step("vector", [mm], (lambda og=og, TW=TW: (lambda e: e.tensor_scalar(
                        out=scrA[:, og, 0:TW], in0=pR[og % 4][:, 0:TW],
                        scalar1=convw_s[:, og, 3:4], scalar2=convb_s[:, og:og + 1],
                        op0=ALU.mult, op1=ALU.add)))())
                else:
                    ev = step("scalar", [mm], (lambda og=og, TW=TW: (lambda e: e.activation(
                        out=scrB[:, og - 16, 0:TW], in_=pR[og % 4][:, 0:TW], func=AF.Silu)))())
                prevA[f"minev{og}_{ti}"] = ev
            uev = [prevA[f"minev{og}_{ti}"] for og in range(32)]
            if ti == 0:
                idb_dump(7, [uev[-1]])
            xp1 = step("tensor", uev[:16] + [v_wbc], (lambda TW=TW: (lambda e: [
                e.matmul(out=pX[0:1, 0:TW], lhsT=wBC_b[:, c, 0:1], rhs=scrA[:, c, 0:TW],
                         start=(c == 0), stop=(c == 15)) for c in range(16)][-1]))())
            xv1 = step("vector", [xp1], (lambda TW=TW: (lambda e: e.tensor_copy(out=sB_s[0:1, 0:TW], in_=pX[0:1, 0:TW])))())
            xp2 = step("tensor", [xv1], (lambda TW=TW: (lambda e: [
                e.matmul(out=pX[0:1, 0:TW], lhsT=wBC_b[:, c, 1:2], rhs=scrA[:, c, 0:TW],
                         start=(c == 0), stop=(c == 15)) for c in range(16)][-1]))())
            xv2 = step("vector", [xp2], (lambda TW=TW: (lambda e: e.tensor_mul(
                out=ysc_s[0:1, 0:TW], in0=sB_s[0:1, 0:TW], in1=pX[0:1, 0:TW])))())
            xbc = step("tensor", [xv2], (lambda TW=TW: (lambda e: e.matmul(
                out=pX[:, 0:TW], lhsT=ones1_s[:], rhs=ysc_s[0:1, 0:TW], start=True, stop=True)))())
            yops = []
            for c in range(16):
                yv1 = step("vector", [xbc] + ([yops[-1]] if yops else []),
                           (lambda c=c, TW=TW: (lambda e: e.tensor_mul(
                               out=scrA[:, c, 0:TW], in0=scrA[:, c, 0:TW], in1=scrB[:, c, 0:TW])))())
                yv = step("vector", [yv1],
                          (lambda c=c, TW=TW: (lambda e: e.tensor_mul(
                              out=scrA[:, c, 0:TW], in0=scrA[:, c, 0:TW], in1=pX[:, 0:TW])))())
                yops.append(yv)
            moev = []
            for op_ in range(4):
                for kc in range(16):
                    buf = (op_ * 16 + kc) % 2
                    ldeps = [yops[-1]]
                    key = f"mo_mm{op_ * 16 + kc - 2}_{ti}"
                    if key in prevA:
                        ldeps.append(prevA[key])
                    if "mos_dbg" in prevA:
                        ldeps.append(prevA.pop("mos_dbg"))
                    lmo = step("gpsimd", ldeps, (lambda op_=op_, kc=kc, buf=buf: (lambda e: e.dma_start(
                        out=mos[:, buf, :],
                        in_=moutw[kc * 128:(kc + 1) * 128, op_ * 256:(op_ + 1) * 256])))(), inc=16)
                    if ti == 0 and kc == 0 and op_ >= 2:
                        idb_dump(8 + op_, [lmo])
                    mdeps = [lmo] + [yops[-1]]
                    if kc == 0 and moev:
                        mdeps.append(moev[-1])
                    mm = step("tensor", mdeps, (lambda op_=op_, kc=kc, buf=buf, TW=TW: (lambda e: (
                        e.matmul(out=pM[0][:, 0:TW], lhsT=mos[:, buf, 0:128], rhs=scrA[:, kc, 0:TW],
                                 start=(kc == 0), stop=(kc == 15)),
                        e.matmul(out=pM[1][:, 0:TW], lhsT=mos[:, buf, 128:256], rhs=scrA[:, kc, 0:TW],
                                 start=(kc == 0), stop=(kc == 15)))[-1]))())
                    prevA[f"mo_mm{op_ * 16 + kc}_{ti}"] = mm
                    if ti == 0 and op_ == 2 and kc >= 8:
                        idb_dump(kc - 8, [mm])
                evp = vseq([prevA[f"mo_mm{op_ * 16 + 15}_{ti}"]], [
                    (lambda op_=op_, c0=c0, TW=TW: (lambda e: e.tensor_add(
                        out=moeacc[:, 2 * op_, c0:c0 + TW], in0=pM[0][:, 0:TW],
                        in1=tokfm[:, 2 * op_, 0:TW])))(),
                    (lambda op_=op_, c0=c0, TW=TW: (lambda e: e.tensor_add(
                        out=moeacc[:, 2 * op_ + 1, c0:c0 + TW], in0=pM[1][:, 0:TW],
                        in1=tokfm[:, 2 * op_ + 1, 0:TW])))()])
                moev.append(evp)
                prevA[f"moev_last_{ti}"] = evp
                if ti == 0 and op_ >= 2:
                    idb_dump(14 + op_, [evp])
            if ti == 0:
                prevA["ep8"] = idb_dump(8, [moev[-1]])
            stm = step("tensor", moev[:] + [CONSTS], (lambda c0=c0, TW=TW: (lambda e: [
                e.matmul(out=pX[0:1, 0:TW], lhsT=ones128b[:], rhs=moeacc[:, oc, c0:c0 + TW],
                         start=(oc == 0), stop=(oc == KC - 1)) for oc in range(KC)][-1]))())
            sqv = []
            for oc in range(KC):
                sdeps = [moev[oc // 2]]
                if sqv:
                    sdeps.append(sqv[-1])
                sq1 = step("scalar", sdeps, (lambda oc=oc, c0=c0, TW=TW: (lambda e: e.activation(
                    out=scrB.bitcast(F32)[:, 0:TW] if False else gsq[:, 0:TW], in_=moeacc[:, oc, c0:c0 + TW],
                    func=AF.Square)))())
                qm = step("tensor", [sq1, stm] + ([sqv[-1]] if sqv else []),
                          (lambda oc=oc, TW=TW: (lambda e: e.matmul(
                              out=pW[0:1, 0:TW], lhsT=ones128[:], rhs=gsq[:, 0:TW],
                              start=(oc == 0), stop=(oc == KC - 1))))())
                sqv.append(qm)
            lnm = vseq([sqv[-1], stm], [
                (lambda TW=TW: (lambda e: e.tensor_scalar(out=stat_m[0:1, 0:TW], in0=pX[0:1, 0:TW], scalar1=1.0 / D, scalar2=None, op0=ALU.mult)))(),
                (lambda TW=TW: (lambda e: e.tensor_scalar(out=stat_r[0:1, 0:TW], in0=pW[0:1, 0:TW], scalar1=1.0 / D, scalar2=None, op0=ALU.mult)))(),
                (lambda TW=TW: (lambda e: e.tensor_mul(out=ysc_s[0:1, 0:TW], in0=stat_m[0:1, 0:TW], in1=stat_m[0:1, 0:TW])))(),
                (lambda TW=TW: (lambda e: e.tensor_sub(out=stat_r[0:1, 0:TW], in0=stat_r[0:1, 0:TW], in1=ysc_s[0:1, 0:TW])))(),
                (lambda TW=TW: (lambda e: e.tensor_scalar(out=stat_r[0:1, 0:TW], in0=stat_r[0:1, 0:TW], scalar1=float(EPS), scalar2=None, op0=ALU.add)))(),
                (lambda TW=TW: (lambda e: e.reciprocal(out=stat_r[0:1, 0:TW], in_=stat_r[0:1, 0:TW])))()])
            lns = step("scalar", [lnm], (lambda TW=TW: (lambda e: e.activation(
                out=stat_r[0:1, 0:TW], in_=stat_r[0:1, 0:TW], func=AF.Sqrt)))())
            lnb = step("tensor", [lns], (lambda TW=TW: (lambda e: (
                e.matmul(out=pX[:, 0:TW], lhsT=ones1_s[:], rhs=stat_m[0:1, 0:TW], start=True, stop=True),
                e.matmul(out=pW[:, 0:TW], lhsT=ones1_s[:], rhs=stat_r[0:1, 0:TW], start=True, stop=True))[-1]))())
            n2v = []
            for oc in range(KC):
                extra = [prevA[f"n2st_{ti - 1}"]] if (ti > 0 and oc == 0) else []
                nv = vseq([lnb] + ([n2v[-1]] if n2v else []) + ld_ln2[-1:] + extra, [
                    (lambda oc=oc, c0=c0, TW=TW: (lambda e: e.tensor_sub(
                        out=gsq[:, 0:TW], in0=moeacc[:, oc, c0:c0 + TW], in1=pX[:, 0:TW])))(),
                    (lambda TW=TW: (lambda e: e.tensor_mul(
                        out=gsq[:, 0:TW], in0=gsq[:, 0:TW], in1=pW[:, 0:TW])))(),
                    (lambda oc=oc, TW=TW: (lambda e: e.tensor_scalar(
                        out=n2str[:, oc, 0:TW], in0=gsq[:, 0:TW],
                        scalar1=ln2w_s[:, oc:oc + 1], scalar2=ln2b_s[:, oc:oc + 1],
                        op0=ALU.mult, op1=ALU.add)))()])
                n2v.append(nv)
            stn2 = step("sync", [n2v[-1]], (lambda c0=c0, TW=TW: (lambda e: e.dma_start(
                out=sc_n2[:, :, c0:c0 + TW], in_=n2str[:, :, 0:TW])))(), inc=16)
            prevA[f"n2st_{ti}"] = stn2
            rmm = step("tensor", [n2v[-1], ld_moer, CONSTS], (lambda TW=TW: (lambda e: [
                e.matmul(out=pX[0:8, 0:TW], lhsT=moer_b[:, kc, :], rhs=n2str[:, kc, 0:TW],
                         start=(kc == 0), stop=(kc == KC - 1)) for kc in range(KC)][-1]))())
            vbias = step("vector", [rmm, ld_moerb], (lambda TW=TW: (lambda e: e.tensor_scalar(
                out=lgt_fm[0:8, 0:TW], in0=pX[0:8, 0:TW], scalar1=moerb_s[0:8, 0:1],
                scalar2=None, op0=ALU.add)))())
            nsub_ = TW // 128
            gts = []
            pendr = None
            lastr = None
            for su in range(nsub_):
                bank = pM[su % 2]
                rdeps = [vbias]
                if su >= 2:
                    rdeps.append(gts[su - 2])
                trl = step("tensor", rdeps, (lambda su=su, bank=bank: (lambda e: e.transpose(
                    out=bank[:, 0:8], in_=lgt_fm[0:8, su * 128:(su + 1) * 128],
                    identity=idf_s[0:8, 0:8])))())
                lastr = trl
                if pendr is not None:
                    gts.append(step("vector", [trl], pendr))
                pendr = (lambda su=su, bank=bank: (lambda e: e.tensor_copy(
                    out=lgt[:, su, :], in_=bank[:, 0:8])))()
            dmyr = step("tensor", [lastr] + ([gts[-1]] if gts else []),
                        lambda e: e.transpose(out=pM[nsub_ % 2][:, 0:128],
                                              in_=idf_s[:, 0:128], identity=idf_s[:]))
            gts.append(step("vector", [dmyr], pendr))
            sexp = step("scalar", [gts[-1]], (lambda nsub_=nsub_: (lambda e: e.activation(
                out=lex[:, 0:nsub_, :], in_=lgt[:, 0:nsub_, :], func=AF.Exp)))())
            gmath = None
            for su in range(nsub_):
                mk = lambda f, su=su: f
                gmath = vseq([sexp] + ([gmath] if gmath else []), [
                    (lambda su=su: (lambda e: e.reduce_sum(out=lsum[:, su:su + 1], in_=lex[:, su, :], axis=AX.X)))(),
                    (lambda su=su: (lambda e: e.reciprocal(out=lsum[:, su:su + 1], in_=lsum[:, su:su + 1])))(),
                    (lambda su=su: (lambda e: e.tensor_scalar(out=lex[:, su, :], in0=lex[:, su, :], scalar1=lsum[:, su:su + 1], scalar2=None, op0=ALU.mult)))(),
                    (lambda su=su: (lambda e: e.reduce_max(out=lm1[:, su:su + 1], in_=lex[:, su, :], axis=AX.X)))(),
                    (lambda su=su: (lambda e: e.tensor_scalar(out=ltmp[:, su, :], in0=lex[:, su, :], scalar1=lm1[:, su:su + 1], scalar2=None, op0=ALU.is_ge)))(),
                    (lambda su=su: (lambda e: e.tensor_scalar(out=ltmp[:, su, :], in0=ltmp[:, su, :], scalar1=-1.0, scalar2=1.0, op0=ALU.mult, op1=ALU.add)))(),
                    (lambda su=su: (lambda e: e.tensor_mul(out=ltmp[:, su, :], in0=lex[:, su, :], in1=ltmp[:, su, :])))(),
                    (lambda su=su: (lambda e: e.reduce_max(out=lm2[:, su:su + 1], in_=ltmp[:, su, :], axis=AX.X)))(),
                    (lambda su=su: (lambda e: e.tensor_add(out=lden[:, su:su + 1], in0=lm1[:, su:su + 1], in1=lm2[:, su:su + 1])))(),
                    (lambda su=su: (lambda e: e.reciprocal(out=lden[:, su:su + 1], in_=lden[:, su:su + 1])))(),
                    (lambda su=su: (lambda e: e.tensor_scalar(out=ltmp[:, su, :], in0=lex[:, su, :], scalar1=lm2[:, su:su + 1], scalar2=None, op0=ALU.is_ge)))(),
                    (lambda su=su: (lambda e: e.tensor_mul(out=ltmp[:, su, :], in0=lex[:, su, :], in1=ltmp[:, su, :])))(),
                    (lambda su=su: (lambda e: e.tensor_scalar(out=ltmp[:, su, :], in0=ltmp[:, su, :], scalar1=lden[:, su:su + 1], scalar2=None, op0=ALU.mult)))()])
            gfms = []
            pendg = None
            lastg = None
            for su in range(nsub_):
                bank = pM[su % 2]
                gdeps_ = [gmath]
                if su >= 2:
                    gdeps_.append(gfms[su - 2])
                trg = step("tensor", gdeps_, (lambda su=su, bank=bank: (lambda e: e.transpose(
                    out=bank[0:8, 0:128], in_=ltmp[:, su, :], identity=idf_s[:])))())
                lastg = trg
                if pendg is not None:
                    gfms.append(step("vector", [trg], pendg))
                pendg = (lambda su=su, c0=c0, bank=bank: (lambda e: e.tensor_copy(
                    out=gate_fm[0:8, c0 + su * 128:c0 + (su + 1) * 128], in_=bank[0:8, 0:128])))()
            dmyg = step("tensor", [lastg] + ([gfms[-1]] if gfms else []),
                        lambda e: e.transpose(out=pM[nsub_ % 2][:, 0:128],
                                              in_=idf_s[:, 0:128], identity=idf_s[:]))
            gfms.append(step("vector", [dmyg], pendg))
            gfm = gfms[-1]
            prevA[f"gate_{ti}"] = gfm
            prevA["tr_ev"] = gfm
            if _dbg2 and ti == 1:
                dn1 = step("sync", [tr_all], lambda e: e.dma_start(out=d_n1, in_=n1fm[:]), inc=16)
                dtk = step("sync", [dn1], lambda e: e.dma_start(out=d_tokf, in_=tokfm[:]), inc=16)
                prevA["tr_ev"] = step("vector", [gfm, dtk],
                                      lambda e: e.tensor_copy(out=widxf[:], in_=j0c[:]))
        last_n2st = prevA[f"n2st_{len(TILES) - 1}"]
        last_gate = prevA[f"gate_{len(TILES) - 1}"]
        d_moe_st = None
        if _dbg2:
            d_moe_st = step("sync", [last_n2st], lambda e: e.dma_start(out=d_moe, in_=moeacc[:]), inc=16)
            step("sync", [last_gate, d_moe_st], lambda e: e.dma_start(out=d_gate, in_=gate_fm[:]), inc=16)
        pb = {}
        for e_ in range(E):
            w1deps = [last_n2st, last_gate]
            if e_ == 0:
                w1deps.append(prevA["tr_ev"])
            else:
                w1deps.append(pb["last_h1mm"])
            ldw1 = step("gpsimd", w1deps, (lambda e_=e_: (lambda e: e.dma_start(
                out=wbig[:, :, 0:H1], in_=moew1[e_].rearrange("(kc p) n -> p kc n", p=128))))(), inc=16)
            ldb1 = None
            for c in range(32):
                ldb1 = step("sync", w1deps if c == 0 else [ldb1],
                            (lambda e_=e_, c=c: (lambda e: e.dma_start(
                                out=b1e_s[:, c:c + 1],
                                in_=moeb1[e_, c * 128:(c + 1) * 128].unsqueeze(1))))(), inc=16)
            ldb2 = None
            for c in range(KC):
                ldb2 = step("sync", [ldb1] if c == 0 else [ldb2],
                            (lambda e_=e_, c=c: (lambda e: e.dma_start(
                                out=b2e_s[:, c:c + 1],
                                in_=moeb2[e_, c * 128:(c + 1) * 128].unsqueeze(1))))(), inc=16)
            for ti, TW in enumerate(TILES):
                c0 = tilestart[ti]
                nsub_ = TW // 128
                n2deps = [last_n2st]
                if "n2ld" in pb:
                    n2deps.append(pb["rmm_last"])
                ldn2 = step("gpsimd", n2deps, (lambda c0=c0, TW=TW: (lambda e: e.dma_start(
                    out=n2str[:, :, 0:TW], in_=sc_n2[:, :, c0:c0 + TW])))(), inc=16)
                pb["n2ld"] = ldn2
                gbdeps = [last_gate, CONSTS]
                if "gb_used" in pb:
                    gbdeps.append(pb["gb_used"])
                tgb = step("tensor", gbdeps, (lambda e_=e_, c0=c0, TW=TW: (lambda e: e.matmul(
                    out=pX[:, 0:TW], lhsT=oneh_s[:, e_ * 128:(e_ + 1) * 128],
                    rhs=gate_fm[0:8, c0:c0 + TW], start=True, stop=True)))())
                vgb = step("vector", [tgb], (lambda TW=TW: (lambda e: e.tensor_copy(
                    out=gatebc[:, 0:TW], in_=pX[:, 0:TW])))())
                h1ev = []
                for og in range(32):
                    hdeps = [ldn2, ldw1]
                    if len(h1ev) >= 4:
                        hdeps.append(h1ev[og - 4])
                    elif "h1_prev_last" in pb and not h1ev:
                        hdeps.append(pb["h1_prev_last"])
                    mm = step("tensor", hdeps, (lambda og=og, TW=TW: (lambda e: [
                        e.matmul(out=pR[og % 4][:, 0:TW], lhsT=wbig[:, kc, og * 128:(og + 1) * 128],
                                 rhs=n2str[:, kc, 0:TW], start=(kc == 0), stop=(kc == KC - 1))
                        for kc in range(KC)][-1]))())
                    pb["rmm_last"] = mm
                    pb["last_h1mm"] = mm
                    dst = scrA if og < 16 else scrB
                    gedeps = [mm, ldb1]
                    if og == 0 and "w2mm_done" in pb:
                        gedeps.append(pb["w2mm_done"])
                    ge = step("scalar", gedeps, (lambda og=og, TW=TW, dst=dst: (lambda e: e.activation(
                        out=dst[:, og % 16, 0:TW], in_=pR[og % 4][:, 0:TW], func=AF.Gelu,
                        bias=b1e_s[:, og:og + 1], scale=1.0)))())
                    h1ev.append(ge)
                pb["h1_prev_last"] = h1ev[-1]
                for op_ in range(4):
                    for grp in range(16):
                        buf = (op_ * 16 + grp) % 2
                        wdeps = [h1ev[-1]]
                        key = f"w2mm{op_ * 16 + grp - 2}"
                        if key in pb:
                            wdeps.append(pb[key])
                        lw2 = step("gpsimd", wdeps, (lambda e_=e_, grp=grp, buf=buf: (lambda e: e.dma_start(
                            out=w2s[:, buf, :, :],
                            in_=moew2[e_, grp * 256:(grp + 1) * 256, :].rearrange("(c p) n -> p c n", p=128))))(), inc=16)
                        mdeps = [lw2, h1ev[min(grp * 2 + 1, 31)]]
                        if grp == 0 and f"w2ev{op_ - 1}" in pb:
                            mdeps.append(pb[f"w2ev{op_ - 1}"])
                        mm = step("tensor", mdeps, (lambda op_=op_, grp=grp, buf=buf, TW=TW: (lambda e: [
                            (e.matmul(out=pM[0][:, 0:TW], lhsT=w2s[:, buf, c, 2 * op_ * 128:(2 * op_ + 1) * 128],
                                      rhs=(scrA if grp * 2 + c < 16 else scrB)[:, (grp * 2 + c) % 16, 0:TW],
                                      start=(grp == 0 and c == 0), stop=(grp == 15 and c == 1)),
                             e.matmul(out=pM[1][:, 0:TW], lhsT=w2s[:, buf, c, (2 * op_ + 1) * 128:(2 * op_ + 2) * 128],
                                      rhs=(scrA if grp * 2 + c < 16 else scrB)[:, (grp * 2 + c) % 16, 0:TW],
                                      start=(grp == 0 and c == 0), stop=(grp == 15 and c == 1)))
                            for c in range(2)][-1][-1]))())
                        pb[f"w2mm{op_ * 16 + grp}"] = mm
                        pb["w2mm_done"] = mm
                    ev = vseq([pb[f"w2mm{op_ * 16 + 15}"], vgb, ldb2]
                              + ([d_moe_st] if (d_moe_st is not None and e_ == 0) else []), [
                        (lambda op_=op_, TW=TW: (lambda e: e.tensor_scalar(
                            out=gsq[:, 0:TW], in0=pM[0][:, 0:TW],
                            scalar1=b2e_s[:, 2 * op_:2 * op_ + 1], scalar2=None, op0=ALU.add)))(),
                        (lambda TW=TW: (lambda e: e.tensor_mul(
                            out=gsq[:, 0:TW], in0=gsq[:, 0:TW], in1=gatebc[:, 0:TW])))(),
                        (lambda op_=op_, c0=c0, TW=TW: (lambda e: e.tensor_add(
                            out=moeacc[:, 2 * op_, c0:c0 + TW],
                            in0=moeacc[:, 2 * op_, c0:c0 + TW], in1=gsq[:, 0:TW])))(),
                        (lambda op_=op_, TW=TW: (lambda e: e.tensor_scalar(
                            out=gsq[:, 0:TW], in0=pM[1][:, 0:TW],
                            scalar1=b2e_s[:, 2 * op_ + 1:2 * op_ + 2], scalar2=None, op0=ALU.add)))(),
                        (lambda TW=TW: (lambda e: e.tensor_mul(
                            out=gsq[:, 0:TW], in0=gsq[:, 0:TW], in1=gatebc[:, 0:TW])))(),
                        (lambda op_=op_, c0=c0, TW=TW: (lambda e: e.tensor_add(
                            out=moeacc[:, 2 * op_ + 1, c0:c0 + TW],
                            in0=moeacc[:, 2 * op_ + 1, c0:c0 + TW], in1=gsq[:, 0:TW])))()])
                    pb[f"w2ev{op_}"] = ev
                pb["gb_used"] = pb["w2ev3"]

        moe_done = pb["w2ev3"]

        sc_prev = None
        tr_prev = None
        for ti, TW in enumerate(TILES):
            c0 = tilestart[ti]
            ld_wr = step("sync", [WROW_DONE] + ([tr_prev] if tr_prev is not None else []),
                         (lambda c0=c0, TW=TW: (lambda e: e.dma_start(
                             out=wrow_t[0:1, 0:TW], in_=sc_wrow[c0:c0 + TW].unsqueeze(0))))(), inc=16)
            wb_deps = [moe_done, ld_wr]
            if tr_prev is not None:
                wb_deps.append(tr_prev)
            t_wbc = step("tensor", wb_deps, (lambda c0=c0, TW=TW: (lambda e: e.matmul(
                out=pX[:, 0:TW], lhsT=ones1_s[:], rhs=wrow_t[0:1, 0:TW],
                start=True, stop=True)))())
            v_wsc = step("vector", [t_wbc], (lambda c0=c0, TW=TW: (lambda e: [
                e.tensor_mul(out=moeacc[:, oc, c0:c0 + TW], in0=moeacc[:, oc, c0:c0 + TW],
                             in1=pX[:, 0:TW]) for oc in range(KC)][-1]))())
            for su in range(TW // 128):
                gcol = (c0 + su * 128) // 128
                evs = []
                pendc = None
                last_tt = None
                for kc in range(KC):
                    b = kc % 2
                    bank = pW if b == 0 else pM[0]
                    tdeps = [v_wsc, CONSTS]
                    bk = f"scC_bank{b}"
                    if prevA.get(bk) is not None:
                        tdeps.append(prevA[bk])
                    tt = step("tensor", tdeps, (lambda kc=kc, c0=c0, su=su, bank=bank: (lambda e: e.transpose(
                        out=bank.bitcast(BF16)[:, 0:128],
                        in_=moeacc[:, kc, c0 + su * 128:c0 + (su + 1) * 128], identity=idb_s[:])))())
                    last_tt = tt
                    if pendc is not None:
                        pf, pbnk, extra = pendc
                        ev = step("vector", [tt] + extra, pf)
                        prevA[f"scC_bank{pbnk}"] = ev
                        tr_prev = ev
                        evs.append(ev)
                    evf = (lambda kc=kc, bank=bank: (lambda e: e.tensor_copy(
                        out=gtok[:, kc * 128:(kc + 1) * 128], in_=bank.bitcast(BF16)[:, 0:128])))()
                    pendc = (evf, b, [sc_prev] if (sc_prev and kc == 0) else [])
                dmyc = step("tensor", [last_tt],
                            lambda e: e.transpose(out=pM[1].bitcast(BF16)[:, 0:128],
                                                  in_=idb_s[:, 0:128], identity=idb_s[:]))
                pf, pbnk, extra = pendc
                ev = step("vector", [dmyc] + extra, pf)
                prevA[f"scC_bank{pbnk}"] = ev
                tr_prev = ev
                evs.append(ev)
                scdeps = evs + [SC_DONE]
                if sc_prev is None:
                    flc = step("gpsimd", [moe_done],
                               lambda e: e.dma_start(out=iota128_s[:], in_=iota128_d), inc=16)
                    scdeps.append(flc)
                sc1 = step("gpsimd", scdeps, (lambda gcol=gcol: (lambda e: e.indirect_dma_start(
                    out=sc_S, out_offset=IndirectOffsetOnAxis(ap=swsi[:, gcol:gcol + 1], axis=0),
                    in_=gtok[:], in_offset=None, bounds_check=TROW, oob_is_err=False)))(), inc=16)
                sc_prev = sc1

        st_y = []
        for r in range(TROW // 128):
            ldx = step("sync", [sc_prev] + ([st_y[-1]] if st_y else []),
                       (lambda r=r: (lambda e: e.dma_start(
                           out=gx[:], in_=xrow[r * 128:(r + 1) * 128, :])))(), inc=16)
            ldS = step("sync", [ldx] + [zS[-1]], (lambda r=r: (lambda e: e.dma_start(
                out=gS[:], in_=sc_S[r * 128:(r + 1) * 128, :])))(), inc=16)
            vcmb = vseq([ldS], [
                lambda e: e.tensor_reduce(out=mu_s[:], in_=gS[:], axis=AX.X, op=ALU.add,
                                          apply_absolute_value=True),
                lambda e: e.tensor_scalar(out=mcol[:], in0=mu_s[:], scalar1=0.0, scalar2=None, op0=ALU.is_gt),
                lambda e: e.tensor_scalar(out=mcol[:], in0=mcol[:], scalar1=-1.0, scalar2=1.0,
                                          op0=ALU.mult, op1=ALU.add),
                lambda e: e.tensor_add(out=gS[:], in0=gS[:], in1=gx[:]),
                lambda e: e.tensor_scalar(out=gx[:], in0=gx[:], scalar1=mcol[:, 0:1], scalar2=None, op0=ALU.mult),
                lambda e: e.tensor_add(out=gS[:], in0=gS[:], in1=gx[:]),
                lambda e: e.reduce_sum(out=mu_s[:], in_=gS[:], axis=AX.X),
                lambda e: e.tensor_scalar(out=mu_s[:], in0=mu_s[:], scalar1=1.0 / D, scalar2=None, op0=ALU.mult)])
            scmb = step("scalar", [vcmb], lambda e: e.activation(out=gsq[:], in_=gS[:], func=AF.Square))
            vcmb2 = vseq([scmb], [
                lambda e: e.reduce_sum(out=rs_s[:], in_=gsq[:], axis=AX.X),
                lambda e: e.tensor_scalar(out=rs_s[:], in0=rs_s[:], scalar1=1.0 / D, scalar2=None, op0=ALU.mult),
                lambda e: e.tensor_mul(out=tmp1_s[:], in0=mu_s[:], in1=mu_s[:]),
                lambda e: e.tensor_sub(out=rs_s[:], in0=rs_s[:], in1=tmp1_s[:]),
                lambda e: e.tensor_scalar(out=rs_s[:], in0=rs_s[:], scalar1=float(EPS), scalar2=None, op0=ALU.add),
                lambda e: e.reciprocal(out=rs_s[:], in_=rs_s[:])])
            scmb2 = step("scalar", [vcmb2], lambda e: e.activation(out=rs_s[:], in_=rs_s[:], func=AF.Sqrt))
            vfin = step("vector", [scmb2], lambda e: e.tensor_scalar(
                out=gS[:], in0=gS[:], scalar1=mu_s[:, 0:1], scalar2=rs_s[:, 0:1],
                op0=ALU.subtract, op1=ALU.mult))
            sty = step("sync", [vfin], (lambda r=r: (lambda e: e.dma_start(
                out=y_tm[r * 128:(r + 1) * 128, :], in_=gS[:])))(), inc=16)
            st_y.append(sty)

        block = ctx.enter_context(nc.Block())

        @block.sync
        def _(sync):
            sd.emit_engine("sync", sync, sems)
            sync.wait_ge(sems["sync"], sd.final_value("sync"))

        @block.gpsimd
        def _(gpsimd):
            sd.emit_engine("gpsimd", gpsimd, sems)

        @block.vector
        def _(vector):
            sd.emit_engine("vector", vector, sems)

        @block.scalar
        def _(scalar):
            sd.emit_engine("scalar", scalar, sems)

        @block.tensor
        def _(tensor):
            sd.emit_engine("tensor", tensor, sems)

    return nc



def _to_fm(x_rowmajor):
    return np.ascontiguousarray(x_rowmajor.T)


def run_l1_ffn(x, w1, b1, w2, b2, use_sim=False):
    B, S, _ = x.shape
    N = B * S
    NT = N // NCORES
    nc = build_l1_ffn(NT=NT)
    xf = x.reshape(N, D)
    in_maps = []
    for c in range(NCORES):
        in_maps.append({
            "xT": _to_fm(xf[c * NT:(c + 1) * NT]),
            "w1": w1, "b1": b1, "w2": w2, "b2": b2,
        })
    if use_sim:
        from concourse.bass_interp import CoreSim
        outs = []
        for c in range(NCORES):
            sim = CoreSim(nc, require_finite=True, require_nnan=True,
                          container_offset=0, rank0_offset=0)
            for k, v in in_maps[c].items():
                sim.tensor(k)[:] = v
            sim.simulate(check_with_hw=False)
            outs.append(np.array(sim.tensor("yT")))
            if c == 0:
                break
        y0 = outs[0]
        return y0
    res = run_bass_kernel_spmd(nc, in_maps, core_ids=list(range(NCORES)))
    y = np.concatenate([np.asarray(r["yT"]).T for r in res.results], axis=0)
    return y.reshape(B, S, D)


def kernel(**inputs):
    ins = {k: np.asarray(v) for k, v in inputs.items()}
    X = ins["L0_attn_x"].astype(np.float32)
    B, S, D_ = X.shape
    assert (B, S, D_) == (4, 8192, 1024)
    HALF = S // 2

    nc0 = build_l0_attn(TROW=S, CAP=2176)
    w0 = dict(
        mw1=ins["L0_attn_modr_w1"], mb1=ins["L0_attn_modr_b1"],
        mw2=ins["L0_attn_modr_w2"], mb2=ins["L0_attn_modr_b2"],
        ln1w=ins["L0_attn_ln1_w"], ln1b=ins["L0_attn_ln1_b"],
        ln2w=ins["L0_attn_ln2_w"], ln2b=ins["L0_attn_ln2_b"],
        minw=ins["L0_attn_m_in_w"], convw=ins["L0_attn_m_conv_w"],
        convb=ins["L0_attn_m_conv_b"], xprojw=ins["L0_attn_m_xproj_w"],
        moutw=ins["L0_attn_m_out_w"], moerw=ins["L0_attn_moe_r_w"],
        moerb=ins["L0_attn_moe_r_b"], moew1=ins["L0_attn_moe_w1"],
        moeb1=ins["L0_attn_moe_b1"], moew2=ins["L0_attn_moe_w2"],
        moeb2=ins["L0_attn_moe_b2"])
    in_maps = []
    for c in range(NCORES):
        r, h = c // 2, c % 2
        hm = np.zeros((128, 1), np.float32)
        hm[64 * h:64 * (h + 1)] = 1.0
        in_maps.append({"xrow": X[r], "xrowT": _to_fm(X[r]),
                        "halfmask": hm,
                        "hoff": np.full((128, 1), h * HALF, np.int32), **w0})
    res = run_bass_kernel_spmd(nc0, in_maps, core_ids=list(range(NCORES)))
    x1 = np.empty((B, S, D), np.float32)
    for c in range(NCORES):
        r, h = c // 2, c % 2
        x1[r, h * HALF:(h + 1) * HALF] = np.asarray(res.results[c]["y_tm"])[h * HALF:(h + 1) * HALF]

    NT = B * S // NCORES
    nc1 = build_l0_ffn(NT=NT)
    wgT = np.ascontiguousarray(ins["L0_ffn_w_gate"].T)
    wuT = np.ascontiguousarray(ins["L0_ffn_w_up"].T)
    wdT = np.ascontiguousarray(ins["L0_ffn_w_down"].T)
    x1f = x1.reshape(B * S, D)
    in_maps = [{"xT": _to_fm(x1f[c * NT:(c + 1) * NT]),
                "wgT": wgT, "wuT": wuT, "wdT": wdT} for c in range(NCORES)]
    res = run_bass_kernel_spmd(nc1, in_maps, core_ids=list(range(NCORES)))
    x2T = []
    for r in range(B):
        x2T.append(np.concatenate([np.asarray(res.results[2 * r + j]["yT"])
                                   for j in range(2)], axis=1))

    nc2 = build_l1_attn(T=S)
    Wq, Wk, Wv = ins["L1_attn_W_q"], ins["L1_attn_W_k"], ins["L1_attn_W_v"]
    WgK, WgC = ins["L1_attn_Wg_K"], ins["L1_attn_Wg_C"]
    bgK, bgC = ins["L1_attn_bg_K"], ins["L1_attn_bg_C"]
    in_maps = []
    for c in range(NCORES):
        r, hg = c // 2, c % 2
        sl = slice(hg * 512, (hg + 1) * 512)
        in_maps.append({"xT": x2T[r],
                        "wq": np.ascontiguousarray(Wq[:, sl]),
                        "wk": np.ascontiguousarray(Wk[:, sl]),
                        "wv": np.ascontiguousarray(Wv[:, sl]),
                        "wgk": np.ascontiguousarray(WgK[:, sl]),
                        "wgc": np.ascontiguousarray(WgC[:, sl]),
                        "bgk": np.ascontiguousarray(bgK[sl]),
                        "bgc": np.ascontiguousarray(bgC[sl])})
    res = run_bass_kernel_spmd(nc2, in_maps, core_ids=list(range(NCORES)))
    oT = []
    for r in range(B):
        oT.append(np.concatenate([np.asarray(res.results[2 * r + j]["oT"])
                                  for j in range(2)], axis=0))

    nc3 = build_l1_attn2(NT=NT)
    in_maps = []
    for c in range(NCORES):
        r, h = c // 2, c % 2
        sl = slice(h * HALF, (h + 1) * HALF)
        in_maps.append({"oT": np.ascontiguousarray(oT[r][:, sl]),
                        "xT": np.ascontiguousarray(x2T[r][:, sl]),
                        "wo": ins["L1_attn_W_o"]})
    res = run_bass_kernel_spmd(nc3, in_maps, core_ids=list(range(NCORES)))
    x3T = [np.asarray(r_["yT"]) for r_ in res.results]

    nc4 = build_l1_ffn(NT=NT)
    in_maps = [{"xT": x3T[c], "w1": ins["L1_ffn_w1"], "b1": ins["L1_ffn_b1"],
                "w2": ins["L1_ffn_w2"], "b2": ins["L1_ffn_b2"]}
               for c in range(NCORES)]
    res = run_bass_kernel_spmd(nc4, in_maps, core_ids=list(range(NCORES)))
    y = np.concatenate([np.asarray(r_["yT"]).T for r_ in res.results], axis=0)
    return np.ascontiguousarray(y.reshape(B, S, D))
